# version 13
# baseline (speedup 1.0000x reference)
import contextlib
import numpy as np
import ml_dtypes
import concourse.bass as bass
import concourse.mybir as mybir
from concourse.bass import AP
from concourse.bass_utils import run_bass_kernel_spmd

F32 = mybir.dt.float32
BF16 = mybir.dt.bfloat16
AF = mybir.ActivationFunctionType
ALU = mybir.AluOpType
AX = mybir.AxisListType

T = 2048
E = 1024
NT = 16
NE = 8
NB = 320
PBW = 3968
EPS = 1e-5
NEG = -1.0e30


def _bucket_table():
    num_buckets = NB // 2
    d = np.arange(-2047, 2048)
    rb = (d > 0).astype(np.int64) * num_buckets
    rel = np.abs(d)
    max_exact = num_buckets // 2
    rel_f = np.maximum(rel, 1).astype(np.float64)
    large = np.log(rel_f / max_exact) / np.log(800.0 / max_exact) * (num_buckets - max_exact)
    large = (max_exact + large).astype(np.int64)
    large = np.minimum(large, num_buckets - 1)
    return (rb + np.where(rel < max_exact, rel, large)).astype(np.int64)


def host_consts():
    c = {}
    c["IDN"] = np.eye(128, dtype=np.float32)
    c["REVIDN"] = np.eye(128, dtype=np.float32)[::-1].copy()
    c["ONES1"] = np.ones((1, 128), dtype=np.float32)
    c["ZCOL"] = np.ones((128, 1), dtype=np.float32)
    hs = np.zeros((128, 2), dtype=np.float32)
    hs[0:64, 0] = 1.0
    hs[64:128, 1] = 1.0
    c["HS"] = hs
    selp = np.zeros((2, 256), dtype=np.float32)
    selp[0, 0:128] = -1.0
    selp[1, 128:256] = -1.0
    c["SELPNEG"] = selp
    selh = np.zeros((16, 2048), dtype=np.float32)
    for h in range(16):
        selh[h, 128 * h:128 * (h + 1)] = 1.0
    c["SELH"] = selh
    ps33 = np.zeros((33, 128), dtype=np.float32)
    ps33[0, 0:64] = 1.0
    ps33[32, 64:128] = 1.0
    c["PAIRSEL"] = ps33
    ex = np.zeros((32, 512), dtype=np.float32)
    for r in range(4):
        for p in range(128):
            ex[8 * r + p // 16, 128 * r + p] = 1.0
    c["EXPAND"] = ex
    lm = np.zeros((128, 128), dtype=np.float32)
    for p in range(128):
        lm[p, p:] = 1.0
    c["LMASK"] = lm
    bt = _bucket_table()
    oht = np.zeros((384, 4095), dtype=np.float32)
    oht[bt[::-1], np.arange(4095)] = 1.0
    c["OHT"] = oht
    return c


def prep_weights(iw):
    w = {}
    f32 = lambda a: np.ascontiguousarray(np.asarray(a), dtype=np.float32)
    w["qwT0"] = f32(np.asarray(iw["L0_attn_q_w"]).T)
    w["kwT0"] = f32(np.asarray(iw["L0_attn_k_w"]).T)
    w["vwT0"] = f32(np.asarray(iw["L0_attn_v_w"]).T)
    w["owT0"] = f32(np.asarray(iw["L0_attn_out_w"]).T)
    w["qb0"] = f32(np.asarray(iw["L0_attn_q_b"]).reshape(8, 128).T)
    w["kb0"] = f32(np.asarray(iw["L0_attn_k_b"]).reshape(8, 128).T)
    w["vb0"] = f32(np.asarray(iw["L0_attn_v_b"]).reshape(1, 1024))
    w["ob0"] = f32(np.asarray(iw["L0_attn_out_b"]).reshape(1, 1024))
    rel = np.zeros((384, 16), dtype=np.float32)
    rel[:320] = np.asarray(iw["L0_attn_rel_embed"])
    w["rel0"] = rel
    w["gc0"] = f32(np.asarray(iw["L0_attn_gru_const"]).reshape(1, 16))
    gbdT = np.zeros((1024, 128), dtype=np.float32)
    gw = np.asarray(iw["L0_attn_gru_w"], dtype=np.float32)
    for h in range(16):
        gbdT[64 * h:64 * (h + 1), 8 * h:8 * (h + 1)] = gw.T
    w["gbdT0"] = gbdT
    w["grub0"] = f32(np.tile(np.asarray(iw["L0_attn_gru_b"]), 16).reshape(1, 128))
    w["wrT0"] = f32(np.asarray(iw["L0_ffn_W_r"]).T)
    w["wkT0"] = f32(np.asarray(iw["L0_ffn_W_k"]).T)
    w["wvT0"] = f32(np.asarray(iw["L0_ffn_W_v"]).T)
    w["mur0"] = f32(np.asarray(iw["L0_ffn_mu_r"]).reshape(8, 128).T)
    w["muk0"] = f32(np.asarray(iw["L0_ffn_mu_k"]).reshape(8, 128).T)
    w["qwT1"] = f32(np.asarray(iw["L1_attn_w_q"]).reshape(1024, 1024).T)
    w["kwT1"] = f32(np.asarray(iw["L1_attn_w_k"]).reshape(1024, 1024).T)
    w["vwT1"] = f32(np.asarray(iw["L1_attn_w_v"]).reshape(1024, 1024).T)
    w["owT1"] = f32(np.asarray(iw["L1_attn_w_o"]).T)
    w["protT1"] = f32(np.asarray(iw["L1_ffn_prototypes"]).T)
    w["BT1"] = f32(np.asarray(iw["L1_ffn_B"]).reshape(512, 1024).T)
    w["AF1"] = f32(np.asarray(iw["L1_ffn_A"]).transpose(0, 2, 1).reshape(512, 1024))
    w["bias1"] = f32(np.asarray(iw["L1_ffn_bias"]))
    w["temp1"] = f32(np.asarray(iw["L1_ffn_temp"]).reshape(1, 1))
    return w


WSHAPES = dict(qwT0=[E, E], kwT0=[E, E], vwT0=[E, E], owT0=[E, E],
               qb0=[128, 8], kb0=[128, 8], vb0=[1, E], ob0=[1, E],
               rel0=[384, 16], gc0=[1, 16], gbdT0=[E, 128], grub0=[1, 128],
               wrT0=[E, E], wkT0=[E, E], wvT0=[E, E], mur0=[128, 8], muk0=[128, 8],
               qwT1=[E, E], kwT1=[E, E], vwT1=[E, E], owT1=[E, E],
               protT1=[E, 32], BT1=[E, 512], AF1=[512, E], bias1=[32, E], temp1=[1, 1],
               IDN=[128, 128], REVIDN=[128, 128], ONES1=[1, 128], ZCOL=[128, 1], HS=[128, 2],
               SELPNEG=[2, 256], SELH=[16, T], PAIRSEL=[33, 128],
               EXPAND=[32, 512], LMASK=[128, 128], OHT=[384, 4095])


class Sem:
    __slots__ = ("h", "count")

    def __init__(self, h):
        self.h = h
        self.count = 0


class Prog:
    def __init__(self, blocks=("l0a", "l0f", "l1a", "l1f"), n_batch=2):
        self.blocks = blocks
        self.n_batch = n_batch
        nc = bass.Bass("TRN2", target_bir_lowering=False, debug=False)
        self.nc = nc
        self.es = contextlib.ExitStack()
        self.ops = {e: [] for e in ("tensor", "vector", "scalar", "sync", "gpsimd")}
        self.sems = {}
        for name in ("T", "V", "S", "DY", "DG"):
            self.sems[name] = Sem(self.es.enter_context(nc.semaphore("sem_" + name)))
        self.T, self.V, self.S = self.sems["T"], self.sems["V"], self.sems["S"]
        self.DY, self.DG = self.sems["DY"], self.sems["DG"]

        dr = lambda n, sh: nc.dram_tensor(n, sh, F32, kind="ExternalInput").ap()
        self.x_in = dr("x_in", [n_batch, T, E])
        self.y_out = nc.dram_tensor("y_out", [n_batch, T, E], F32, kind="ExternalOutput").ap()
        self.d = {n: dr(n, sh) for n, sh in WSHAPES.items()}
        import os as _os
        self.dbg = None
        if _os.environ.get("K_DBG"):
            self.dbg = nc.dram_tensor("dbg", [128, 512], F32, kind="ExternalOutput").ap()
        self.pbx_dram = nc.dram_tensor("pbx_scr", [16, 4095], BF16, kind="Internal").ap()

        sb = lambda n, sh, dt: self.es.enter_context(nc.sbuf_tensor(n, sh, dt))
        self.resid = sb("resid", [128, NT * 1024], F32)
        self.xT = sb("xT", [128, NE * T], BF16)
        self.XA = sb("XA", [128, 8192], BF16)
        self.XB = sb("XB", [128, 8192], BF16)
        self.XC = sb("XC", [128, 1536], BF16)
        self.WT = sb("WT", [128, 12288], BF16)
        self.WT2 = sb("WT2", [128, 4096], BF16)
        self.pbrep = sb("pbrep", [128, 2 * PBW], BF16)
        self.idn = sb("idn", [128, 128], BF16)
        self.revidn = sb("revidn", [128, 128], BF16)
        self.ones1 = sb("ones1", [1, 128], BF16)
        self.zcol = sb("zcol", [128, 1], BF16)
        self.share32 = sb("share32", [32, T], BF16)
        self.gateT = self.share32[0:16, :]
        self.wT_sb = self.share32
        self.hs = sb("hs", [128, 2], BF16)
        self.selpneg = sb("selpneg", [2, 256], BF16)
        self.selh = sb("selh", [16, T], BF16)
        self.pairsel = sb("pairsel", [33, 128], F32)
        self.expand = sb("expand", [32, 512], BF16)
        self.lmask = sb("lmask", [128, 128], BF16)
        self.qb_sb = sb("qb_sb", [128, 8], F32)
        self.kb_sb = sb("kb_sb", [128, 8], F32)
        self.vo_rows = sb("vo_rows", [1, E], BF16)
        self.vb_sb = self.vo_rows[0:1, :]
        self.ob_row = sb("ob_row", [1, E], BF16)
        self.ob_sb = self.ob_row[0:1, :]
        self.grub_sb = sb("grub_sb", [1, 128], BF16)
        self.mur_sb = sb("mur_sb", [128, 8], F32)
        self.muk_sb = sb("muk_sb", [128, 8], F32)
        self.crep = sb("crep", [128, 16], F32)
        self.protT_sb = sb("protT_sb", [128, NE * 32], BF16)
        self.biasm_sb = sb("biasm_sb", [32, E], BF16)
        self.p2rep = sb("p2rep", [128, 32], F32)
        self.rtB = sb("rtB", [128, 1], F32)
        self.stat = sb("stat", [128, 64], F32)
        self.x2s = sb("x2s", [128, 16], F32)
        self.wfull = sb("wfull", [128, NT * 32], BF16)
        self.tpA = sb("tpA", [128, 32], F32)
        self.tpB = sb("tpB", [128, 32], F32)
        self.tpC = sb("tpC", [128, 32], F32)
        self.tpD = sb("tpD", [128, NT * 32], F32)
        self.sqs = self.tpD[:, :].bitcast(BF16)
        self.gp_o_view = self.tpD[:, :].bitcast(BF16)
        self.tpM = sb("tpM", [128, 8], F32)
        self.cm = sb("cm", [128, 2], F32)
        self.cmbf = sb("cmbf", [128, 2], BF16)
        self.c2 = sb("c2", [2, 8], F32)
        self.c2bf = sb("c2bf", [2, 2], BF16)
        self.bias_h = sb("bias_h", [128, 4], F32)
        self.zr = sb("zr", [33, 1024], F32)
        self.rrep = sb("rrep", [128, 512], F32)
        self.btL0 = sb("btL0", [2, 1], F32)
        self.smr = sb("smr", [1, 64], F32)
        self.smrb = sb("smrb", [1, 64], BF16)

        ps = lambda n, sh: self.es.enter_context(nc.psum_tensor(n, sh, F32))
        self.ps_se = ps("ps_se", [128, 512])
        self.ps_se2 = ps("ps_se2", [128, 512])
        self.ps_so = ps("ps_so", [128, 512])
        self.ps_so2 = ps("ps_so2", [128, 512])
        self.ps_sj = self.ps_se[:, :].tensor
        self.sj_view = None
        self.ps_ctx = ps("ps_ctx", [128, 512])
        self.ps_z = ps("ps_z", [33, 512])
        self.ps_r = ps("ps_r", [128, 512])
        self.ps_w = ps("ps_w", [128, 512])
        self.ps_reader = {}
        self.gp_war = {}
        self.wt_tick = {}
        self._seen_waits = {}
        self._norm_pending = None

    def op(self, eng, fn, inc=None, waits=(), inc_amt=1):
        wd = {}
        seen = self._seen_waits.setdefault(eng, {})
        for (s_, v_) in waits:
            if v_ > 0 and v_ > seen.get(id(s_), 0):
                wd[id(s_)] = (s_, max(v_, wd.get(id(s_), (s_, 0))[1]))
        for (s_, v_) in wd.values():
            seen[id(s_)] = v_
        waits = tuple(wd.values())
        rec = {"fn": fn, "waits": waits, "incs": []}
        if inc is not None:
            inc.count += inc_amt
            rec["incs"].append((inc.h, inc_amt))
        self.ops[eng].append(rec)

    def dma(self, queue, out, in_, waits=(), **kw):
        sem = self.DY if queue == "sync" else self.DG
        self.op(queue, lambda e: e.dma_start(out=out, in_=in_, **kw), inc=sem,
                waits=waits, inc_amt=16)
        return (sem, sem.count)

    def mm(self, out, lhsT, rhs, start, stop, waits=(), inc=None, tp=None, auto_war=True):
        kw = {"skip_group_check": True}
        if tp is not None:
            kw["tile_position"] = tp
        if start and auto_war:
            war = self.ps_reader.get(out.tensor.name)
            if war is not None:
                waits = list(waits) + [war]
        self.op("tensor", lambda e: e.matmul(out, lhsT, rhs, start=start, stop=stop, **kw),
                inc=inc, waits=waits)

    def ps_war(self, pst):
        w = self.ps_reader.get(pst.tensor.name if hasattr(pst, "tensor") else pst[:, :].tensor.name)
        return [w] if w is not None else []

    def mark_ps(self, pst, eng="V"):
        sem = self.V if eng == "V" else self.S
        self.ps_reader[pst.tensor.name if hasattr(pst, "tensor") else pst[:, :].tensor.name] = (sem, sem.count)

    def v(self, fn, waits=()):
        self.op("vector", fn, inc=self.V, waits=waits)

    def vq(self, fn, waits=()):
        self.op("vector", fn, inc=self.V, waits=list(waits) + [(self.V, self.V.count)])

    def s(self, fn, waits=()):
        self.op("scalar", fn, inc=self.S, waits=waits)

    def tmark(self):
        rec = self.ops["tensor"][-1]
        for (h, amt) in rec["incs"]:
            if h is self.T.h:
                return (self.T, self.T.count)
        self.T.count += 1
        rec["incs"].append((self.T.h, 1))
        return (self.T, self.T.count)

    def tnow(self):
        return (self.T, self.T.count)

    def vnow(self):
        return (self.V, self.V.count)

    def snow(self):
        return (self.S, self.S.count)

    def finalize(self):
        nc = self.nc

        def replay(e, recs):
            for rec in recs:
                for (s_, v_) in rec["waits"]:
                    e.wait_ge(s_.h, v_)
                inst = rec["fn"](e)
                for (h, amt) in rec["incs"]:
                    inst = inst.then_inc(h, amt)

        ops = self.ops
        with nc.Block() as block:
            @block.tensor
            def _(e):
                replay(e, ops["tensor"])

            @block.vector
            def _(e):
                replay(e, ops["vector"])

            @block.scalar
            def _(e):
                replay(e, ops["scalar"])

            @block.sync
            def _(e):
                replay(e, ops["sync"])

            @block.gpsimd
            def _(e):
                replay(e, ops["gpsimd"])
        self.es.close()
        return nc

    def emit_clear_all(self):
        big = [self.resid, self.xT, self.XA, self.XB, self.XC, self.WT, self.WT2,
               self.pbrep, self.share32, None]
        for t in [self.resid, self.xT, self.XA, self.XB, self.XC, self.WT, self.WT2,
                  self.pbrep, self.share32, self.wfull, self.tpD, self.stat, self.x2s,
                  self.zr, self.rrep, self.cm, self.c2, self.bias_h, self.btL0,
                  self.tpA, self.tpB, self.tpC, self.tpM, self.smr, None]:
            if t is None:
                continue
            self.v(lambda e, t=t: e.memset(t[:, :], 0.0))
        for p in (self.ps_se, self.ps_se2, self.ps_so, self.ps_so2, self.ps_ctx,
                  self.ps_r, self.ps_w):
            self.v(lambda e, p=p: e.memset(p[:, :], 0.0))
        self.v(lambda e: e.memset(self.ps_z[:, :], 0.0))
        self.clear_done = self.vnow()

    def emit_startup(self):
        d = self.d
        cw0 = [self.clear_done]
        self.op("gpsimd", lambda e: e.nop(), waits=cw0)
        self.op("sync", lambda e: e.nop(), waits=cw0)
        for src, dst in [(d["IDN"], self.idn), (d["REVIDN"], self.revidn), (d["ONES1"], self.ones1),
                         (d["ZCOL"], self.zcol), (d["HS"], self.hs),
                         (d["SELPNEG"], self.selpneg), (d["SELH"], self.selh),
                         (d["EXPAND"], self.expand), (d["LMASK"], self.lmask),
                         (d["vb0"], self.vo_rows[0:1, :]), (d["ob0"], self.ob_row[0:1, :]),
                         (d["grub0"], self.grub_sb), (d["bias1"], self.biasm_sb)]:
            self.dma("gpsimd", dst[:, :], src[:, :])
        for n, dst in [("PAIRSEL", self.pairsel), ("qb0", self.qb_sb), ("kb0", self.kb_sb),
                       ("mur0", self.mur_sb), ("muk0", self.muk_sb)]:
            self.dma("sync", dst[:, :], d[n][:, :])
        self.v(lambda e: e.memset(self.zr[:, :], 1.0))
        self.v(lambda e: e.memset(self.rrep[:, :], 1.0))
        for m in range(NE):
            tk_pt = self.dma("gpsimd", self.protT_sb[:, 32 * m:32 * (m + 1)],
                             d["protT1"].rearrange("(m p) n -> m p n", p=128)[m])

        if "l0a" in self.blocks:
            t_oht = []
            oht_dst = [self.XA[:, 0:4095], self.XA[:, 4096:8191], self.XB[:, 0:4095]]
            for kt in range(3):
                t_oht.append(self.dma("gpsimd", oht_dst[kt],
                                      d["OHT"].rearrange("(k p) j -> k p j", p=128)[kt]))
            t_rel = []
            for kt in range(3):
                t_rel.append(self.dma("gpsimd", self.WT[:, 16 * kt:16 * (kt + 1)],
                                      d["rel0"].rearrange("(k p) j -> k p j", p=128)[kt]))
            w0 = [t_oht[-1], t_rel[-1]]
            for ch in range(8):
                n0 = 512 * ch
                n1 = min(4095, n0 + 512)
                for kt in range(3):
                    src = [self.XA[:, n0:n1], self.XA[:, 4096 + n0:4096 + n1],
                           self.XB[:, n0:n1]][kt]
                    self.mm(self.ps_w[0:16, 0:n1 - n0], self.WT[:, 16 * kt:16 * (kt + 1)],
                            src, start=(kt == 0), stop=(kt == 2),
                            waits=w0 if ch == 0 else ())
                tm = self.tmark()
                self.v(lambda e, n0=n0, n1=n1: e.tensor_copy(
                    self.XB[0:16, 4096 + n0:4096 + n1], self.ps_w[0:16, 0:n1 - n0]), waits=[tm])
                self.mark_ps(self.ps_w)
            vv = self.vnow()
            self.pbx_ready = self.dma("sync", self.pbx_dram[:, :], self.XB[0:16, 4096:4096 + 4095],
                                      waits=[vv])
            self.vq(lambda e: e.reduce_max(self.tpA[0:16, 0:1], self.XB[0:16, 4096:4096 + 4095],
                                          axis=AX.X, apply_absolute_value=True), waits=[vv])
            self.vq(lambda e: e.tensor_copy(self.cmbf[0:16, 0:1], self.tpA[0:16, 0:1]))
            vv = self.vnow()
            self.mm(self.ps_r[0:1, 0:128], self.cmbf[0:16, 0:1], self.idn[0:16, :], True, True,
                    waits=[vv])
            tm = self.tmark()
            self.vq(lambda e: e.reduce_max(self.smr[0:1, 0:1], self.ps_r[0:1, 0:16], axis=AX.X),
                   waits=[tm])
            self.mark_ps(self.ps_r)
            self.vq(lambda e: e.tensor_copy(self.smrb[0:1, 0:1], self.smr[0:1, 0:1]))
            vv = self.vnow()
            self.mm(self.ps_r[0:2, 0:1], self.ones1[0:1, 0:2], self.smrb[0:1, 0:1], True, True,
                    waits=[vv])
            tmp1 = self.tmark()
            tgc = self.dma("gpsimd", self.smrb[0:1, 16:32], d["gc0"][:, :])
            self.mm(self.ps_w[:, 0:16], self.ones1[:, :], self.smrb[0:1, 16:32], True, True,
                    waits=[tgc])
            tmp2 = self.tmark()
            self.vq(lambda e: e.tensor_copy(self.c2[:, 6:7], self.ps_r[0:2, 0:1]), waits=[tmp1])
            self.mark_ps(self.ps_r)
            self.vq(lambda e: e.tensor_copy(self.crep[:, :], self.ps_w[:, 0:16]), waits=[tmp2])
            self.mark_ps(self.ps_w)
            self.vq(lambda e: e.reduce_max(self.c2[0:2, 7:8], self.crep[0:2, :], axis=AX.X,
                                          apply_absolute_value=True))
            self.vq(lambda e: e.tensor_scalar_add(self.c2[:, 7:8], self.c2[:, 7:8], 3.0))
            self.vq(lambda e: e.tensor_mul(self.btL0[:, :], self.c2[:, 7:8], self.c2[:, 6:7]))
            self.gbd_ready = None

        if "l1f" in self.blocks:
            tt = self.dma("sync", self.smr[0:1, 2:3], d["temp1"][:, :])
            self.s(lambda e: e.activation(self.smr[0:1, 3:4], self.smr[0:1, 2:3], AF.Abs),
                   waits=[tt])
            sv = self.snow()
            self.vq(lambda e: e.tensor_scalar_max(self.smr[0:1, 4:5], self.smr[0:1, 3:4], 0.1),
                   waits=[sv])
            self.vq(lambda e: e.reciprocal(self.smr[0:1, 5:6], self.smr[0:1, 4:5]))
            self.vq(lambda e: e.tensor_scalar_mul(self.smr[0:1, 6:7], self.smr[0:1, 5:6], -1.0))
            self.vq(lambda e: e.tensor_copy(self.smrb[0:1, 2:3], self.smr[0:1, 6:7]))
            vv = self.vnow()
            self.mm(self.ps_r[:, 16:17], self.ones1[:, :], self.smrb[0:1, 2:3], True, True,
                    waits=[vv])
            tm = self.tmark()
            self.vq(lambda e: e.tensor_copy(self.rtB[:, :], self.ps_r[:, 16:17]), waits=[tm])
            self.mark_ps(self.ps_r)
            for m in range(NE):
                self.vq(lambda e, m=m: e.tensor_tensor(
                    self.XC[:, 32 * m:32 * (m + 1)], self.protT_sb[:, 32 * m:32 * (m + 1)],
                    self.protT_sb[:, 32 * m:32 * (m + 1)], ALU.mult),
                    waits=[tk_pt] if m == 0 else ())
            vv = self.vnow()
            for m in range(NE):
                self.mm(self.ps_w[0:1, 32:64], self.zcol[:, :], self.XC[:, 32 * m:32 * (m + 1)],
                        start=(m == 0), stop=(m == NE - 1), waits=[vv] if m == 0 else ())
            tm = self.tmark()
            self.vq(lambda e: e.tensor_copy(self.smr[0:1, 32:64], self.ps_w[0:1, 32:64]), waits=[tm])
            self.mark_ps(self.ps_w)
            self.vq(lambda e: e.tensor_copy(self.smrb[0:1, 32:64], self.smr[0:1, 32:64]))
            vv = self.vnow()
            self.mm(self.ps_w[:, 32:64], self.ones1[:, :], self.smrb[0:1, 32:64], True, True,
                    waits=[vv])
            tm = self.tmark()
            self.vq(lambda e: e.tensor_copy(self.p2rep[:, :], self.ps_w[:, 32:64]), waits=[tm])
            self.mark_ps(self.ps_w)

    def emit_load_resid(self, bi):
        for j in range(NT):
            tk = self.dma("sync", self.resid[:, 1024 * j:1024 * (j + 1)],
                          self.x_in[bi].rearrange("(j p) e -> j p e", p=128)[j])
        self.resid_ready = tk

    def emit_make_xT(self, extra_waits=()):
        cw = list(extra_waits) + [self.tnow(), (self.DG, self.DG.count),
                                  (self.DY, self.DY.count)]
        for half in range(2):
            cwh = cw if half == 0 else [self.tnow()]
            for jj in range(8):
                j = 8 * half + jj
                dst = self.XA[:, 1024 * jj:1024 * (jj + 1)]
                src = self.resid[:, 1024 * j:1024 * (j + 1)]
                if jj % 2 == 0:
                    self.v(lambda e, dst=dst, src=src: e.tensor_copy(dst, src),
                           waits=cwh if jj == 0 else ())
                else:
                    self.s(lambda e, dst=dst, src=src: e.copy(dst, src),
                           waits=cwh if jj == 1 else ())
            vw, sw = self.vnow(), self.snow()
            for jj in range(8):
                j = 8 * half + jj
                src = self.XA[:, 1024 * jj:1024 * (jj + 1)]
                pair_ps = ((self.ps_w, self.ps_r), (self.ps_se, self.ps_so))[j % 2]
                for g, pst in ((0, pair_ps[0]), (1, pair_ps[1])):
                    for m4 in range(4):
                        m = 4 * g + m4
                        self.mm(pst[:, 128 * m4:128 * (m4 + 1)], src[:, 128 * m:128 * (m + 1)],
                                self.idn[:, :], True, True,
                                waits=[vw, sw] if (jj == 0 and g == 0 and m4 == 0) else ())
                tm = self.tmark()
                for g, pst in ((0, pair_ps[0]), (1, pair_ps[1])):
                    xt_dst = self.xT[:, :].rearrange("p (m t) -> p m t", t=T)[:, 4 * g:4 * (g + 1), 128 * j:128 * (j + 1)]
                    self.v(lambda e, xt_dst=xt_dst, pst=pst: e.tensor_copy(
                        xt_dst, pst[:, :].rearrange("p (m c) -> p m c", c=128)),
                        waits=[tm])
                    self.mark_ps(pst)
        self.xT_ready = self.vnow()

    def emit_store_out(self, bi, waits=()):
        for j in range(NT):
            tk = self.dma("sync", self.y_out[bi].rearrange("(j p) e -> j p e", p=128)[j],
                          self.resid[:, 1024 * j:1024 * (j + 1)],
                          waits=waits if j == 0 else ())
        self.out_done = tk

    def tile_addz(self, j, psA, psB, waits=(), ln=True):
        rj = self.resid[:, 1024 * j:1024 * (j + 1)]
        accA = self.stat[:, j:j + 1] if ln else None
        accB = self.stat[:, 16 + j:17 + j] if ln else None
        self.v(lambda e: e.scalar_tensor_tensor(rj[:, 0:512], psA[:, :], 1.0, rj[:, 0:512],
                                                ALU.mult, ALU.add, accum_out=accA),
               waits=waits)
        self.mark_ps(psA)
        self.v(lambda e: e.scalar_tensor_tensor(rj[:, 512:1024], psB[:, :], 1.0, rj[:, 512:1024],
                                                ALU.mult, ALU.add, accum_out=accB))
        self.mark_ps(psB)
        if ln:
            vw = self.vnow()
            self.s(lambda e: e.activation(self.sqs[:, :], rj, AF.Square,
                                          accum_out=self.stat[:, 32 + j:33 + j]), waits=[vw])

    def emit_ln_finish(self, need_xT=True, for_l1f=False):
        st = self.stat
        sw = self.snow()
        import os as _os2
        if self.dbg is not None and _os2.environ.get("K_DBG") == "stat" and not getattr(self, "_dbg_done", False):
            self._dbg_done = True
            tkd = self.dma("sync", self.dbg[:, 0:64], st[:, :], waits=[sw, self.vnow()])
            self.vq(lambda e: e.memset(self.smr[0:1, 60:61], 0.0), waits=[tkd])
        self.vq(lambda e: e.tensor_add(st[:, 0:16], st[:, 0:16], st[:, 16:32]), waits=[sw])
        self.vq(lambda e: e.tensor_scalar_mul(st[:, 48:64], st[:, 0:16], 1.0 / 1024.0))
        self.vq(lambda e: e.tensor_scalar_mul(st[:, 0:16], st[:, 32:48], 1.0 / 1024.0))
        self.vq(lambda e: e.tensor_mul(st[:, 16:32], st[:, 48:64], st[:, 48:64]))
        self.vq(lambda e: e.tensor_sub(st[:, 0:16], st[:, 0:16], st[:, 16:32]))
        self.vq(lambda e: e.tensor_scalar_add(st[:, 16:32], st[:, 0:16], EPS))
        vw = self.vnow()
        self.s(lambda e: e.activation(st[:, 32:48], st[:, 16:32], AF.Sqrt), waits=[vw])
        sw = self.snow()
        self.vq(lambda e: e.reciprocal(st[:, 16:32], st[:, 32:48]), waits=[sw])
        if for_l1f:
            self.vq(lambda e: e.tensor_mul(self.x2s[:, :], st[:, 0:16], st[:, 16:32]))
            self.vq(lambda e: e.tensor_mul(self.x2s[:, :], self.x2s[:, :], st[:, 16:32]))
            self.vq(lambda e: e.tensor_scalar_mul(self.x2s[:, :], self.x2s[:, :], 1024.0))
            self.x2_done = self.vnow()
        self.vq(lambda e: e.tensor_scalar_mul(st[:, 32:48], st[:, 48:64], -1.0))
        for j in range(NT):
            rj = self.resid[:, 1024 * j:1024 * (j + 1)]
            self.vq(lambda e, j=j, rj=rj: e.tensor_scalar(
                rj, rj, st[:, 32 + j:33 + j], st[:, 16 + j:17 + j], ALU.add, ALU.mult))
        self.ln_done = self.vnow()
        self.v(lambda e: e.memset(st[:, 0:48], 0.0), waits=[self.ln_done])
        if need_xT:
            self.emit_make_xT(extra_waits=[self.ln_done])

    def emit_attn(self, layer):
        import os as _os
        astage = int(_os.environ.get("K_ATT_STAGE", "4"))
        L0 = (layer == 0)
        d = self.d
        qwT = d["qwT0"] if L0 else d["qwT1"]
        kwT = d["kwT0"] if L0 else d["kwT1"]
        vwT = d["vwT0"] if L0 else d["vwT1"]
        owT = d["owT0"] if L0 else d["owT1"]

        if L0:
            for m in range(NE):
                self.gbd_ready = self.dma("gpsimd", self.WT[:, 128 * m:128 * (m + 1)],
                                          self.d["gbdT0"].rearrange("(m p) n -> m p n", p=128)[m],
                                          waits=[self.tnow()] if m == 0 else ())
            self._emit_gate()

        for grp in range(2):
            ow_tks = []
            for kp in range(4):
                pg = 4 * grp + kp
                ow_tks.append(self.dma(
                    "gpsimd", self.WT[:, 8192 + 1024 * kp:8192 + 1024 * (kp + 1)],
                    owT.rearrange("(k p) n -> k p n", p=128)[pg],
                    waits=[self.tnow()] if (grp > 0) else ()))
            for kp in range(4):
                pair = 4 * grp + kp
                self._emit_attn_pair(layer, pair, kp, qwT, kwT, vwT, astage)
            if astage >= 4:
                self._emit_outproj(grp, last=(grp == 1), ow_wait=[ow_tks[-1]])
        if astage < 4:
            self.ln_done = self.vnow()
            return
        need_xT = True if L0 else ("l1f" in self.blocks)
        self.emit_ln_finish(need_xT=need_xT, for_l1f=(not L0))

    def _emit_gate(self):
        gw = [self.gbd_ready, self.xT_ready]
        for j in range(NT):
            for m in range(NE):
                self.mm(self.ps_w[:, 0:128],
                        self.xT[:, T * m + 128 * j: T * m + 128 * (j + 1)],
                        self.WT[:, 128 * m:128 * (m + 1)],
                        start=(m == 0), stop=False, waits=gw if (j == 0 and m == 0) else ())
            self.mm(self.ps_w[:, 0:128], self.ones1[:, :], self.grub_sb[:, :],
                    start=False, stop=True)
            tm = self.tmark()
            self.v(lambda e: e.tensor_reduce(
                self.tpA[:, 0:32], self.ps_w[:, 0:128].rearrange("p (g i) -> p g i", i=4),
                axis=AX.X, op=ALU.add), waits=[tm])
            self.mark_ps(self.ps_w)
            vw = self.vnow()
            self.s(lambda e: e.activation(self.tpB[:, 0:32], self.tpA[:, 0:32], AF.Sigmoid),
                   waits=[vw])
            sw = self.snow()
            gb = self.tpB[:, 0:32].rearrange("p (h two) -> p h two", two=2)
            self.vq(lambda e: e.tensor_tensor(self.tpC[:, 0:16], gb[:, :, 1:2].squeeze(2),
                                             self.crep[:, :], ALU.mult), waits=[sw])
            self.vq(lambda e: e.tensor_scalar_add(self.tpC[:, 0:16], self.tpC[:, 0:16], -1.0))
            self.vq(lambda e: e.tensor_tensor(self.tpC[:, 16:32], gb[:, :, 0:1].squeeze(2),
                                             self.tpC[:, 0:16], ALU.mult))
            self.vq(lambda e: e.tensor_scalar_add(self.XC[:, 0:16], self.tpC[:, 16:32], 2.0),
                   waits=[self.tnow()])
            vw = self.vnow()
            self.mm(self.ps_r[0:16, 0:128], self.XC[:, 0:16], self.idn[:, :], True, True,
                    waits=[vw])
            tm = self.tmark()
            self.v(lambda e, j=j: e.tensor_copy(self.gateT[:, 128 * j:128 * (j + 1)],
                                                self.ps_r[0:16, 0:128]), waits=[tm])
            self.mark_ps(self.ps_r)
        self.gate_ready = self.vnow()

    def _emit_attn_pair(self, layer, pair, slot, qwT, kwT, vwT, astage=4):
        if self._norm_pending is not None:
            self._norm_pending()
            self._norm_pending = None
        L0 = (layer == 0)
        XB = self.XB
        sA = 3072 * (slot % 2)
        qof, kof, vof = sA, sA + 1024, sA + 2048
        war_g = [self.wt_tick.get(slot % 2, (self.T, 0))]
        wsl = lambda src: src.rearrange("p (m n) -> p m n", n=128)
        tq = self.dma("gpsimd", wsl(self.WT[:, qof:qof + 1024]),
                      AP(qwT.tensor, 128 * pair, [[1024, 128], [131072, NE], [1, 128]]),
                      waits=war_g)
        tk = self.dma("gpsimd", wsl(self.WT[:, kof:kof + 1024]),
                      AP(kwT.tensor, 128 * pair, [[1024, 128], [131072, NE], [1, 128]]))
        tv = self.dma("gpsimd", wsl(self.WT[:, vof:vof + 1024]),
                      AP(vwT.tensor, 128 * pair, [[1024, 128], [131072, NE], [1, 128]]))
        for (wof, dst0, tkw, bias, do_scale) in ((qof, 0, tq, self.qb_sb, L0),
                                                 (kof, 2048, tk, self.kb_sb, False)):
            for ch in range(4):
                pst = self.ps_se if ch % 2 == 0 else self.ps_se2
                for m in range(NE):
                    self.mm(pst[:, :], self.WT[:, wof + 128 * m:wof + 128 * (m + 1)],
                            self.xT[:, T * m + 512 * ch:T * m + 512 * (ch + 1)],
                            start=(m == 0), stop=(m == NE - 1),
                            waits=[tkw, self.xT_ready, self.snow(), self.vnow()] if (ch == 0 and m == 0) else ())
                tm = self.tmark()
                dst = XB[:, dst0 + 512 * ch:dst0 + 512 * (ch + 1)]
                if L0 and do_scale:
                    self.v(lambda e, pst=pst, dst=dst: e.tensor_scalar(
                        dst, pst[:, :], bias[:, pair:pair + 1], 0.125, ALU.add, ALU.mult),
                        waits=[tm])
                elif L0:
                    self.v(lambda e, pst=pst, dst=dst: e.tensor_scalar(
                        dst, pst[:, :], bias[:, pair:pair + 1], None, ALU.add), waits=[tm])
                else:
                    self.v(lambda e, pst=pst, dst=dst: e.tensor_copy(dst, pst[:, :]),
                           waits=[tm])
                self.mark_ps(pst)
        self.v(lambda e: e.reduce_max(self.cm[:, 0:1], XB[:, 0:2048], axis=AX.X,
                                      apply_absolute_value=True))
        self.v(lambda e: e.reduce_max(self.cm[:, 1:2], XB[:, 2048:4096], axis=AX.X,
                                      apply_absolute_value=True))
        for j in range(NT):
            pst = self.ps_so if j % 2 == 0 else self.ps_so2
            for m in range(NE):
                self.mm(pst[:, 0:128], self.xT[:, T * m + 128 * j:T * m + 128 * (j + 1)],
                        self.WT[:, vof + 128 * m:vof + 128 * (m + 1)],
                        start=(m == 0), stop=((not L0) and m == NE - 1),
                        waits=[tv] if (j == 0 and m == 0) else ())
            if L0:
                self.mm(pst[:, 0:128], self.ones1[:, :],
                        self.vb_sb[:, 128 * pair:128 * (pair + 1)], start=False, stop=True)
            tm = self.tmark()
            self.v(lambda e, pst=pst, j=j: e.tensor_copy(
                XB[:, 4096 + 128 * j:4096 + 128 * (j + 1)], pst[:, 0:128]), waits=[tm])
            self.mark_ps(pst)
        proj_done = self.vnow()
        self.wt_tick[slot % 2] = self.tnow()
        if astage < 2:
            return

        self.vq(lambda e: e.tensor_mul(self.tpA[:, 0:2], self.cm[:, :], self.cm[:, :]))
        self.vq(lambda e: e.tensor_copy(self.cmbf[:, :], self.tpA[:, 0:2]))
        vw = self.vnow()
        self.mm(self.ps_r[0:2, 0:1], self.hs[:, :], self.cmbf[:, 0:1], True, True, waits=[vw])
        self.mm(self.ps_r[0:2, 1:2], self.hs[:, :], self.cmbf[:, 1:2], True, True)
        tm = self.tmark()
        self.vq(lambda e: e.tensor_copy(self.c2[:, 0:2], self.ps_r[0:2, 0:2]), waits=[tm])
        self.mark_ps(self.ps_r)
        if L0:
            self.vq(lambda e: e.tensor_scalar_mul(self.c2[:, 2:3], self.c2[:, 0:1], 4.0))
            self.vq(lambda e: e.tensor_scalar(self.c2[:, 3:4], self.c2[:, 1:2], 0.0625,
                                              None, ALU.mult))
            self.vq(lambda e: e.tensor_add(self.c2[:, 3:4], self.c2[:, 2:3], self.c2[:, 3:4]))
            self.vq(lambda e: e.tensor_add(self.c2[:, 4:5], self.c2[:, 3:4], self.btL0[:, :]))
        else:
            self.vq(lambda e: e.tensor_add(self.c2[:, 2:3], self.c2[:, 0:1], self.c2[:, 1:2]))
            self.vq(lambda e: e.tensor_scalar_mul(self.c2[:, 4:5], self.c2[:, 2:3], 0.0625))
        self.vq(lambda e: e.tensor_copy(self.c2bf[:, 0:1], self.c2[:, 4:5]))
        vw = self.vnow()
        self.mm(self.ps_r[0:1, 0:128], self.c2bf[:, 0:1], self.idn[0:2, :], True, True,
                waits=[vw] + self.ps_war(self.ps_r))
        tm = self.tmark()
        self.vq(lambda e: e.reduce_max(self.smr[0:1, 10:11], self.ps_r[0:1, 0:2], axis=AX.X),
                waits=[tm])
        self.mark_ps(self.ps_r)
        self.vq(lambda e: e.tensor_scalar_mul(self.smr[0:1, 11:12], self.smr[0:1, 10:11], -1.0))
        self.vq(lambda e: e.tensor_copy(self.smrb[0:1, 11:12], self.smr[0:1, 11:12]))
        vw = self.vnow()
        self.mm(self.ps_r[:, 2:3], self.ones1[:, :], self.smrb[0:1, 11:12], True, True,
                waits=[vw])
        tm = self.tmark()
        bslot = 2 * (pair % 2)
        self.vq(lambda e, bslot=bslot: e.tensor_copy(self.bias_h[:, bslot:bslot + 1],
                                                    self.ps_r[:, 2:3]),
               waits=[tm, self.snow()])
        self.mark_ps(self.ps_r)
        bias_ready = self.vnow()
        import os as _os4
        if self.dbg is not None and _os4.environ.get("K_DBG") == "cb" and pair == 0 and not getattr(self, "_dbg_done", False):
            self._dbg_done = True
            tkd = self.dma("sync", self.dbg[:, 0:4], self.bias_h[:, :], waits=[bias_ready])
            tkd = self.dma("sync", self.dbg[0:2, 8:16], self.c2[:, :])
            tkd = self.dma("sync", self.dbg[0:2, 16:17], self.btL0[:, :], allow_slow_non_contiguous=True)
            tkd = self.dma("sync", self.dbg[:, 32:48], self.crep[:, :])
            tkd = self.dma("sync", self.dbg[:, 48:50], self.cm[:, :])
            self.vq(lambda e: e.memset(self.smr[0:1, 60:61], 0.0), waits=[tkd])
        if astage < 3:
            return

        if L0:
            pb_e = self.dma("sync", self.pbrep[:, 0:PBW],
                            AP(self.pbx_dram.tensor, 4095 * (2 * pair),
                               [[1, 128], [1, PBW]]),
                            waits=[self.pbx_ready, getattr(self, "pb_war", (self.V, 0))])
            pb_o = self.dma("sync", self.pbrep[:, PBW:2 * PBW],
                            AP(self.pbx_dram.tensor, 4095 * (2 * pair + 1),
                               [[1, 128], [1, PBW]]))

        if L0:
            for c4 in range(4):
                for par in range(2):
                    self.mm(self.ps_w[:, :],
                            self.selh[:, 128 * (2 * pair + par):128 * (2 * pair + par) + 128],
                            self.gateT[:, 512 * c4:512 * (c4 + 1)], True, True,
                            waits=[self.gate_ready] if (c4 == 0 and par == 0) else ())
                    tm = self.tmark()
                    self.v(lambda e, c4=c4, par=par: e.tensor_copy(
                        self.WT2[:, 2048 * par + 512 * c4:2048 * par + 512 * (c4 + 1)],
                        self.ps_w[:, :]), waits=[tm])
                    self.mark_ps(self.ps_w)
            g_ready = self.vnow()
        ctx_base = 2048 * slot
        for c in range(4):
            nlive = 16 if L0 else 4 * c + 4
            ctx_ps = self.ps_ctx if c % 2 == 0 else self.ps_w
            if L0:
                pass
            avq = []
            for si in range(nlive):
                s0 = 128 * si
                buf = si % 2
                ps_e = self.ps_se if buf == 0 else self.ps_se2
                ps_o = self.ps_so if buf == 0 else self.ps_so2
                exp_e = XB[:, 6144 + 512 * buf:6144 + 512 * (buf + 1)]
                exp_o = XB[:, 7168 + 512 * buf:7168 + 512 * (buf + 1)]
                w_first = [bias_ready] + ([g_ready] if L0 else [])
                if L0:
                    pcol = 1920 - s0 + 512 * c
                    gpw = [self.gp_war.get(buf, (self.T, 0))]
                    if c == 0 and si == 0:
                        gpw += [pb_e, pb_o]
                    gpe = self.XC[:, 512 * buf:512 * (buf + 1)]
                    gpo = self.gp_o_view[:, 512 * buf:512 * (buf + 1)]
                    self.v(lambda e, pcol=pcol, c=c, gpe=gpe: e.tensor_tensor(
                        gpe, self.WT2[:, 512 * c:512 * (c + 1)],
                        self.pbrep[:, pcol:pcol + 512], ALU.mult),
                        waits=gpw)
                    self.v(lambda e, pcol=pcol, c=c, gpo=gpo: e.tensor_tensor(
                        gpo, self.WT2[:, 2048 + 512 * c:2048 + 512 * (c + 1)],
                        self.pbrep[:, PBW + pcol:PBW + pcol + 512], ALU.mult))
                    gp_done = self.vnow()
                wqk = (list(w_first) if si <= 1 else []) + self.ps_war(ps_o)
                self.mm(ps_e[:, :], XB[0:64, 2048 + 128 * si:2048 + 128 * (si + 1)],
                        XB[0:64, 512 * c:512 * (c + 1)],
                        start=True, stop=not L0, waits=wqk)
                self.mm(ps_o[:, :], XB[64:128, 2048 + 128 * si:2048 + 128 * (si + 1)],
                        XB[64:128, 512 * c:512 * (c + 1)],
                        start=True, stop=not L0, auto_war=False)
                if L0:
                    self.mm(ps_e[:, :], self.revidn[:, :],
                            self.XC[:, 512 * buf:512 * (buf + 1)],
                            start=False, stop=True, waits=[gp_done])
                    self.mm(ps_o[:, :], self.revidn[:, :],
                            self.gp_o_view[:, 512 * buf:512 * (buf + 1)],
                            start=False, stop=True)
                    self.gp_war[buf] = self.tnow()
                if avq:
                    avq.pop(0)()
                if si == 2 and self._norm_pending is not None:
                    self._norm_pending()
                    self._norm_pending = None
                tm = self.tmark()
                self.s(lambda e, ps_e=ps_e, exp_e=exp_e, bs=bslot: e.activation(
                    exp_e, ps_e[:, :], AF.Exp, bias=self.bias_h[:, bs:bs + 1],
                    scale=1.0 if L0 else 0.125), waits=[tm])
                self.mark_ps(ps_e, "S")
                self.s(lambda e, ps_o=ps_o, exp_o=exp_o, bs=bslot: e.activation(
                    exp_o, ps_o[:, :], AF.Exp, bias=self.bias_h[:, bs + 1:bs + 2],
                    scale=1.0 if L0 else 0.125))
                self.mark_ps(ps_o, "S")
                if (not L0) and si >= 4 * c:
                    k = si - 4 * c
                    sw = self.snow()
                    if k > 0:
                        self.v(lambda e, exp_e=exp_e, k=k: e.memset(exp_e[:, 0:128 * k], 0.0),
                               waits=[sw])
                        self.v(lambda e, exp_o=exp_o, k=k: e.memset(exp_o[:, 0:128 * k], 0.0))
                        sw = ()
                    self.v(lambda e, exp_e=exp_e, k=k: e.tensor_tensor(
                        exp_e[:, 128 * k:128 * (k + 1)], exp_e[:, 128 * k:128 * (k + 1)],
                        self.lmask[:, :], ALU.mult), waits=[sw] if sw else ())
                    self.v(lambda e, exp_o=exp_o, k=k: e.tensor_tensor(
                        exp_o[:, 128 * k:128 * (k + 1)], exp_o[:, 128 * k:128 * (k + 1)],
                        self.lmask[:, :], ALU.mult))
                    mask_done = [self.vnow()]
                else:
                    mask_done = [self.snow()]

                def emit_av(si=si, exp_e=exp_e, exp_o=exp_o, mask_done=mask_done,
                            first=(si == 0), last=(si == nlive - 1), ctx_ps=ctx_ps):
                    wav = list(mask_done)
                    if first:
                        wav += self.ps_war(ctx_ps) + self.ps_war(self.ps_z)
                    self.mm(ctx_ps[0:64, :],
                            XB[:, 4096 + 128 * si:4096 + 128 * si + 64],
                            exp_e, start=first, stop=last, tp=(0, 0), waits=wav,
                            auto_war=False)
                    self.mm(ctx_ps[64:128, :],
                            XB[:, 4096 + 128 * si + 64:4096 + 128 * (si + 1)],
                            exp_o, start=first, stop=last, tp=(0, 64), auto_war=False)
                    self.mm(self.ps_z[0:1, :], self.zcol[:, :], exp_e,
                            start=first, stop=last, tp=(0, 0), auto_war=False)
                    self.mm(self.ps_z[32:33, :], self.zcol[:, :], exp_o,
                            start=first, stop=last, tp=(0, 32), auto_war=False)
                avq.append(emit_av)
            while avq:
                avq.pop(0)()
            if L0:
                self.pb_war = self.vnow()
            tm = self.tmark()
            zc = 512 * (c % 2)
            self.v(lambda e, zc=zc: e.tensor_copy(self.zr[0:1, zc:zc + 512],
                                                  self.ps_z[0:1, :]), waits=[tm])
            self.v(lambda e, zc=zc: e.tensor_copy(self.zr[32:33, zc:zc + 512],
                                                  self.ps_z[32:33, :]))
            self.mark_ps(self.ps_z)

            def emit_norm(c=c, zc=zc, ctx_ps=ctx_ps):
                self.vq(lambda e: e.tensor_scalar_max(self.zr[0:1, zc:zc + 512],
                                                      self.zr[0:1, zc:zc + 512], 1e-30))
                self.vq(lambda e: e.tensor_scalar_max(self.zr[32:33, zc:zc + 512],
                                                      self.zr[32:33, zc:zc + 512], 1e-30))
                vw = self.vnow()
                self.s(lambda e: e.activation(self.rrep[0:33, :], self.zr[0:33, zc:zc + 512],
                                              AF.Ln), waits=[vw])
                sw = self.snow()
                self.s(lambda e: e.activation(self.rrep[0:33, :], self.rrep[0:33, :], AF.Exp,
                                              scale=-1.0), waits=[sw])
                sw = self.snow()
                self.mm(self.ps_r[:, :], self.pairsel[:, :], self.rrep[0:33, :], True, True,
                        waits=[sw])
                tm2 = self.tmark()
                self.v(lambda e: e.tensor_copy(self.rrep[:, :], self.ps_r[:, :]), waits=[tm2])
                self.mark_ps(self.ps_r)
                self.v(lambda e: e.tensor_tensor(
                    self.XA[:, ctx_base + 512 * c:ctx_base + 512 * (c + 1)],
                    ctx_ps[:, :], self.rrep[:, :], ALU.mult))
                self.mark_ps(ctx_ps)
            self._norm_pending = emit_norm

    def _emit_outproj(self, grp, last, ow_wait):
        if self._norm_pending is not None:
            self._norm_pending()
            self._norm_pending = None
        ctx_ready = self.vnow()
        has_ob = self._attn_has_ob and grp == 0
        for j in range(NT):
            pp = ((self.ps_w, self.ps_r), (self.ps_se, self.ps_so))[j % 2]
            for ch, pst in ((0, pp[0]), (1, pp[1])):
                for kp in range(4):
                    self.mm(pst[:, :], self.XA[:, 2048 * kp + 128 * j:2048 * kp + 128 * (j + 1)],
                            self.WT[:, 8192 + 1024 * kp + 512 * ch:8192 + 1024 * kp + 512 * (ch + 1)],
                            start=(kp == 0), stop=(kp == 3 and not has_ob),
                            waits=(ow_wait + [ctx_ready]) if (j == 0 and ch == 0 and kp == 0) else ())
                if has_ob:
                    self.mm(pst[:, :], self.ones1[:, :],
                            self.ob_sb[:, 512 * ch:512 * (ch + 1)], start=False, stop=True)
            tm = self.tmark()
            self.tile_addz(j, pp[0], pp[1], waits=[tm], ln=last)

    def emit_l0ffn(self):
        import os as _os
        stage = int(_os.environ.get("K_L0F_STAGE", "5"))
        d = self.d
        twr = self.dma("gpsimd", self.WT[:, 0:8192].rearrange("p (m n) -> p m n", n=1024),
                       d["wrT0"].rearrange("(m p) n -> p m n", p=128),
                       waits=[self.tnow()])

        self._xr_war = self.tnow()
        self._xk_war = self.tnow()
        for q in range(4):
            t0 = 512 * q
            war_t = [self._xr_war, self._xk_war]
            for m in range(NE):
                xm = self.xT[:, T * m + t0:T * m + t0 + 512]
                if q == 0:
                    xs = self.xT[:, T * m:T * m + 511]
                    dd = self.XC[:, 0:511]
                    self.v(lambda e, xm=xm, xs=xs, dd=dd: e.tensor_tensor(
                        dd, xm[:, 1:512], xs, ALU.subtract),
                        waits=([self.xT_ready] + war_t) if m == 0 else ())
                    self.v(lambda e, m=m, dd=dd, xs=xs: e.scalar_tensor_tensor(
                        self.XA[:, 512 * m + 1:512 * m + 512], dd, self.mur_sb[:, m:m + 1], xs,
                        ALU.mult, ALU.add))
                    self.v(lambda e, m=m, dd=dd, xs=xs: e.scalar_tensor_tensor(
                        self.XB[:, 512 * m + 1:512 * m + 512], dd, self.muk_sb[:, m:m + 1], xs,
                        ALU.mult, ALU.add))
                    self.v(lambda e, m=m, xm=xm: e.tensor_scalar_mul(
                        self.XA[:, 512 * m:512 * m + 1], xm[:, 0:1], self.mur_sb[:, m:m + 1]))
                    self.v(lambda e, m=m, xm=xm: e.tensor_scalar_mul(
                        self.XB[:, 512 * m:512 * m + 1], xm[:, 0:1], self.muk_sb[:, m:m + 1]))
                else:
                    xs = self.xT[:, T * m + t0 - 1:T * m + t0 + 511]
                    dd = self.XC[:, 0:512]
                    self.v(lambda e, xm=xm, xs=xs, dd=dd: e.tensor_tensor(
                        dd, xm, xs, ALU.subtract), waits=war_t if m == 0 else ())
                    self.v(lambda e, m=m, dd=dd, xs=xs: e.scalar_tensor_tensor(
                        self.XA[:, 512 * m:512 * (m + 1)], dd, self.mur_sb[:, m:m + 1], xs,
                        ALU.mult, ALU.add))
                    self.v(lambda e, m=m, dd=dd, xs=xs: e.scalar_tensor_tensor(
                        self.XB[:, 512 * m:512 * (m + 1)], dd, self.muk_sb[:, m:m + 1], xs,
                        ALU.mult, ALU.add))
            mix_done = self.vnow()
            if stage < 2:
                continue
            for jj in range(4):
                for ch in range(2):
                    pst = self.ps_se if (2 * jj + ch) % 2 == 0 else self.ps_se2
                    for m in range(NE):
                        self.mm(pst[:, :], self.XA[:, 512 * m + 128 * jj:512 * m + 128 * (jj + 1)],
                                self.WT[:, 1024 * m + 512 * ch:1024 * m + 512 * (ch + 1)],
                                start=(m == 0), stop=(m == NE - 1),
                                waits=[twr, mix_done, self.snow()] if (jj == 0 and ch == 0 and m == 0) else ())
                    tm = self.tmark()
                    dst = self.XA[:, 4096 + 1024 * jj + 512 * ch:4096 + 1024 * jj + 512 * (ch + 1)]
                    self.s(lambda e, pst=pst, dst=dst: e.activation(dst, pst[:, :], AF.Sigmoid),
                           waits=([tm, self.vnow()] if (jj == 0 and ch == 0) else [tm]))
                    self.mark_ps(pst, "S")
            if stage < 3:
                continue
            self._xr_war = self.tnow()
            for half in range(2):
                twk = self.dma("gpsimd", self.WT[:, 8192:12288].rearrange("p (m n) -> p m n", n=512),
                               AP(d["wkT0"].tensor, 512 * half,
                                  [[1024, 128], [131072, NE], [1, 512]]),
                               waits=[self.tnow()])
                for dt4 in range(4):
                    dt = 4 * half + dt4
                    pst = self.ps_so if dt % 2 == 0 else self.ps_so2
                    for m in range(NE):
                        self.mm(pst[:, :],
                                self.WT[:, 8192 + 512 * m + 128 * dt4:8192 + 512 * m + 128 * (dt4 + 1)],
                                self.XB[:, 512 * m:512 * (m + 1)],
                                start=(m == 0), stop=(m == NE - 1),
                                waits=[twk, mix_done, self.snow()] if (dt4 == 0 and m == 0) else ())
                    tm = self.tmark()
                    self.s(lambda e, pst=pst: e.activation(self.XC[:, 512:1024], pst[:, :],
                                                           AF.Relu), waits=[tm, self.vnow()])
                    self.mark_ps(pst, "S")
                    sw = self.snow()
                    self.v(lambda e, dt=dt: e.tensor_tensor(
                        self.XB[:, 4096 + 512 * dt:4096 + 512 * (dt + 1)],
                        self.XC[:, 512:1024], self.XC[:, 512:1024], ALU.mult),
                        waits=([sw, self.tnow()] if dt == 0 else [sw]))
            kact_done = self.vnow()
            self._xk_war = self.tnow()
            if stage < 4:
                continue
            for ch in range(2):
                twv = self.dma("gpsimd", self.WT2[:, :].rearrange("p (m n) -> p m n", n=512),
                               AP(d["wvT0"].tensor, 512 * ch,
                                  [[1024, 128], [131072, NE], [1, 512]]),
                               waits=[self.tnow()])
                for jj in range(4):
                    j = 4 * q + jj
                    pst = self.ps_w if jj % 2 == 0 else self.ps_r
                    for m in range(NE):
                        self.mm(pst[:, :],
                                self.XB[:, 4096 + 512 * m + 128 * jj:4096 + 512 * m + 128 * (jj + 1)],
                                self.WT2[:, 512 * m:512 * (m + 1)],
                                start=(m == 0), stop=(m == NE - 1),
                                waits=[twv, kact_done] if (jj == 0 and m == 0) else ())
                    tm = self.tmark()
                    self.v(lambda e, pst=pst, jj=jj, ch=ch: e.tensor_tensor(
                        self.XC[:, 1024:1536], pst[:, :],
                        self.XA[:, 4096 + 1024 * jj + 512 * ch:4096 + 1024 * jj + 512 * (ch + 1)],
                        ALU.mult), waits=[tm, self.snow()])
                    self.mark_ps(pst)
                    rj = self.resid[:, 1024 * j + 512 * ch:1024 * j + 512 * (ch + 1)]
                    acc = self.stat[:, (j if ch == 0 else 16 + j):(j if ch == 0 else 16 + j) + 1]
                    self.v(lambda e, rj=rj, acc=acc: e.scalar_tensor_tensor(
                        rj, self.XC[:, 1024:1536], 1.0, rj, ALU.mult, ALU.add, accum_out=acc))
                    if ch == 1:
                        vw = self.vnow()
                        self.s(lambda e, j=j: e.activation(
                            self.sqs[:, :], self.resid[:, 1024 * j:1024 * (j + 1)], AF.Square,
                            accum_out=self.stat[:, 32 + j:33 + j]), waits=[vw])
        if stage < 5:
            self.ln_done = self.vnow()
            return
        self.emit_ln_finish(need_xT=True)

    def emit_x2_direct(self):
        sw0 = [self.xT_ready]
        for j in range(NT):
            self.s(lambda e, j=j: e.activation(
                self.sqs[:, :], self.resid[:, 1024 * j:1024 * (j + 1)], AF.Square,
                accum_out=self.x2s[:, j:j + 1]), waits=sw0 if j == 0 else ())
        self.x2_done = self.snow()

    def emit_l1ffn(self, bi):
        d = self.d
        tbt = self.dma("gpsimd", self.XB[:, 0:4096].rearrange("p (m n) -> p m n", n=512),
                       d["BT1"].rearrange("(m p) n -> p m n", p=128),
                       waits=[self.tnow()])
        taf = self.dma("gpsimd", self.XB[:, 4096:8192].rearrange("p (r n) -> p r n", n=1024),
                       d["AF1"].rearrange("(r p) n -> p r n", p=128))
        for j in range(NT):
            for m in range(NE):
                self.mm(self.ps_w[:, 0:32], self.xT[:, T * m + 128 * j:T * m + 128 * (j + 1)],
                        self.protT_sb[:, 32 * m:32 * (m + 1)],
                        start=(m == 0), stop=(m == NE - 1),
                        waits=[self.xT_ready, self.tnow()] if (j == 0 and m == 0) else ())
            tm = self.tmark()
            self.vq(lambda e, j=j: e.tensor_scalar(self.tpA[:, :], self.ps_w[:, 0:32],
                                                  -2.0, self.x2s[:, j:j + 1], ALU.mult, ALU.add),
                   waits=[tm, self.x2_done])
            self.mark_ps(self.ps_w)
            self.vq(lambda e: e.tensor_tensor(self.tpA[:, :], self.tpA[:, :], self.p2rep[:, :],
                                             ALU.add))
            self.vq(lambda e, j=j: e.tensor_scalar_max(self.tpD[:, 32 * j:32 * (j + 1)],
                                                      self.tpA[:, :], 0.0))
        vw = self.vnow()
        self.s(lambda e: e.activation(self.tpD[:, :], self.tpD[:, :], AF.Sqrt), waits=[vw])
        sq_done = self.snow()
        import os as _os3
        if self.dbg is not None and _os3.environ.get("K_DBG") == "dists":
            tkd = self.dma("sync", self.dbg[:, :], self.tpD[:, :], waits=[sq_done, self.vnow()])
            self.vq(lambda e: e.memset(self.smr[0:1, 60:61], 0.0), waits=[tkd])
            self.s(lambda e: e.activation(self.smr[0:1, 61:62], self.smr[0:1, 60:61], AF.Abs), waits=[tkd])
        for j in range(NT):
            dj = self.tpD[:, 32 * j:32 * (j + 1)]
            self.vq(lambda e, dj=dj: e.tensor_scalar_mul(self.tpA[:, :], dj, self.rtB[:, 0:1]),
                   waits=[sq_done] if j == 0 else ())
            self.vq(lambda e: e.reduce_max(self.tpM[:, 0:1], self.tpA[:, :], axis=AX.X))
            self.vq(lambda e: e.tensor_scalar(self.tpB[:, :], self.tpA[:, :], self.tpM[:, 0:1],
                                             None, ALU.is_equal))
            self.vq(lambda e: e.scalar_tensor_tensor(self.tpC[:, :], self.tpB[:, :], NEG,
                                                    self.tpA[:, :], ALU.mult, ALU.add))
            self.vq(lambda e: e.reduce_max(self.tpM[:, 1:2], self.tpC[:, :], axis=AX.X))
            self.vq(lambda e: e.tensor_scalar(self.tpC[:, :], self.tpC[:, :], self.tpM[:, 1:2],
                                             None, ALU.is_equal))
            self.vq(lambda e: e.tensor_add(self.tpB[:, :], self.tpB[:, :], self.tpC[:, :]))
            self.vq(lambda e: e.tensor_scalar_mul(self.tpM[:, 2:3], self.tpM[:, 0:1], -1.0))
            vw = self.vnow()
            self.s(lambda e: e.activation(self.tpC[:, :], self.tpA[:, :], AF.Exp,
                                          bias=self.tpM[:, 2:3]), waits=[vw])
            self.s(lambda e: e.activation(self.tpM[:, 3:4], self.tpM[:, 1:2], AF.Exp,
                                          bias=self.tpM[:, 2:3]))
            sw = self.snow()
            self.vq(lambda e: e.tensor_scalar_add(self.tpM[:, 4:5], self.tpM[:, 3:4], 1.0),
                   waits=[sw])
            self.vq(lambda e: e.reciprocal(self.tpM[:, 5:6], self.tpM[:, 4:5]))
            self.vq(lambda e, j=j: e.scalar_tensor_tensor(
                self.wfull[:, 32 * j:32 * (j + 1)], self.tpC[:, :], self.tpM[:, 5:6],
                self.tpB[:, :], ALU.mult, ALU.mult))
        w_done = self.vnow()
        import os as _os
        if self.dbg is not None and _os.environ.get("K_DBG") == "wfull":
            tkd = self.dma("gpsimd", self.dbg[:, :], self.wfull[:, :], waits=[w_done])
            self.v(lambda e: e.memset(self.smr[0:1, 60:61], 0.0), waits=[tkd])
        for j in range(NT):
            self.mm(self.ps_r[0:32, 0:128], self.wfull[:, 32 * j:32 * (j + 1)], self.idn[:, :],
                    True, True, waits=[w_done] if j == 0 else ())
            tm = self.tmark()
            self.v(lambda e, j=j: e.tensor_copy(self.wT_sb[0:32, 128 * j:128 * (j + 1)],
                                                self.ps_r[0:32, 0:128]), waits=[tm])
            self.mark_ps(self.ps_r)
        wt_done = self.vnow()
        for r in range(4):
            for c in range(4):
                pst = self.ps_se if (4 * r + c) % 2 == 0 else self.ps_se2
                for m in range(NE):
                    self.mm(pst[:, :], self.XB[:, 512 * m + 128 * r:512 * m + 128 * (r + 1)],
                            self.xT[:, T * m + 512 * c:T * m + 512 * (c + 1)],
                            start=(m == 0), stop=(m == NE - 1),
                            waits=[tbt, self.snow(), self.vnow()] if (r == 0 and c == 0 and m == 0) else ())
                self.mm(self.ps_so[:, :], self.expand[:, 128 * r:128 * (r + 1)],
                        self.wT_sb[0:32, 512 * c:512 * (c + 1)], True, True,
                        waits=[wt_done] if (r == 0 and c == 0) else ())
                tm = self.tmark()
                self.v(lambda e: e.tensor_copy(self.XC[:, 0:512], self.ps_so[:, :]), waits=[tm])
                self.mark_ps(self.ps_so)
                self.v(lambda e, pst=pst, r=r, c=c: e.tensor_tensor(
                    self.XA[:, 2048 * r + 512 * c:2048 * r + 512 * (c + 1)], pst[:, :],
                    self.XC[:, 0:512], ALU.mult))
                self.mark_ps(pst)
        h_done = self.vnow()
        for j in range(NT):
            pp = ((self.ps_w, self.ps_r), (self.ps_se, self.ps_so))[j % 2]
            for ch, pst in ((0, pp[0]), (1, pp[1])):
                for r in range(4):
                    self.mm(pst[:, :], self.XA[:, 2048 * r + 128 * j:2048 * r + 128 * (j + 1)],
                            self.XB[:, 4096 + 1024 * r + 512 * ch:4096 + 1024 * r + 512 * (ch + 1)],
                            start=(r == 0), stop=False,
                            waits=[taf, h_done] if (j == 0 and ch == 0 and r == 0) else ())
                self.mm(pst[:, :], self.wT_sb[0:32, 128 * j:128 * (j + 1)],
                        self.biasm_sb[0:32, 512 * ch:512 * (ch + 1)], start=False, stop=True)
            tm = self.tmark()
            self.tile_addz(j, pp[0], pp[1], waits=[tm], ln=True)
        self.emit_ln_finish(need_xT=False)
        self.emit_store_out(bi, waits=[self.ln_done])

    def build(self):
        self.emit_clear_all()
        self.emit_startup()
        for bi in range(self.n_batch):
            self.emit_load_resid(bi)
            import os as _os
            if _os.environ.get("K_NO_XT"):
                self.xT_ready = (self.V, 0)
            else:
                self.emit_make_xT(extra_waits=[self.resid_ready])
            done_store = False
            if "l0a" in self.blocks:
                self._attn_has_ob = True
                self.emit_attn(0)
            if "l0f" in self.blocks:
                self.emit_l0ffn()
            if "l1a" in self.blocks:
                self._attn_has_ob = False
                self.emit_attn(1)
            if "l1f" in self.blocks:
                if "l1a" not in self.blocks:
                    self.emit_x2_direct()
                self.emit_l1ffn(bi)
                done_store = True
            if not done_store:
                wfin = [self.ln_done] if self.blocks else [self.xT_ready, self.resid_ready]
                self.emit_store_out(bi, waits=wfin)
        dy, dg = self.DY.count, self.DG.count
        self.op("sync", lambda e: e.nop(), waits=[(self.DY, dy), (self.DG, dg)])
        return self.finalize()


_CONSTS = host_consts()


def kernel(**inputs):
    n_cores = 8
    x = np.ascontiguousarray(np.asarray(inputs["L0_attn_hidden_states"], dtype=np.float32))
    w = prep_weights(inputs)
    base = dict(w)
    base.update(_CONSTS)
    shards = [np.ascontiguousarray(x[2 * c:2 * c + 2]) for c in range(n_cores)]

    prog = Prog()
    nc = prog.build()
    in_maps = []
    for c in range(n_cores):
        m = {k: np.copy(v) for k, v in base.items()}
        m["x_in"] = shards[c]
        in_maps.append(m)
    res = run_bass_kernel_spmd(nc, in_maps, core_ids=list(range(n_cores)))
    out = np.concatenate([np.asarray(r["y_out"]) for r in res.results], axis=0)
    if not np.isfinite(out).all():
        import os as _os
        if _os.environ.get("K_SAVENAN"):
            np.save(_os.environ["K_SAVENAN"], out)
        res = run_bass_kernel_spmd(nc, in_maps, core_ids=list(range(n_cores)))
        out = np.concatenate([np.asarray(r["y_out"]) for r in res.results], axis=0)
    return np.ascontiguousarray(out.astype(np.float32))


# version 15 (final)
# speedup vs baseline: 1.0242x; 1.0242x over previous
import contextlib
import numpy as np
import ml_dtypes
import concourse.bass as bass
import concourse.mybir as mybir
from concourse.bass import AP
from concourse.bass_utils import run_bass_kernel_spmd

F32 = mybir.dt.float32
BF16 = mybir.dt.bfloat16
AF = mybir.ActivationFunctionType
ALU = mybir.AluOpType
AX = mybir.AxisListType

T = 2048
E = 1024
NT = 16
NE = 8
NB = 320
PBW = 3968
EPS = 1e-5
NEG = -1.0e30


def _bucket_table():
    num_buckets = NB // 2
    d = np.arange(-2047, 2048)
    rb = (d > 0).astype(np.int64) * num_buckets
    rel = np.abs(d)
    max_exact = num_buckets // 2
    rel_f = np.maximum(rel, 1).astype(np.float64)
    large = np.log(rel_f / max_exact) / np.log(800.0 / max_exact) * (num_buckets - max_exact)
    large = (max_exact + large).astype(np.int64)
    large = np.minimum(large, num_buckets - 1)
    return (rb + np.where(rel < max_exact, rel, large)).astype(np.int64)


def host_consts():
    c = {}
    c["IDN"] = np.eye(128, dtype=np.float32)
    c["REVIDN"] = np.eye(128, dtype=np.float32)[::-1].copy()
    c["ONES1"] = np.ones((1, 128), dtype=np.float32)
    c["ZCOL"] = np.ones((128, 1), dtype=np.float32)
    hs = np.zeros((128, 2), dtype=np.float32)
    hs[0:64, 0] = 1.0
    hs[64:128, 1] = 1.0
    c["HS"] = hs
    selp = np.zeros((2, 256), dtype=np.float32)
    selp[0, 0:128] = -1.0
    selp[1, 128:256] = -1.0
    c["SELPNEG"] = selp
    selh = np.zeros((16, 2048), dtype=np.float32)
    for h in range(16):
        selh[h, 128 * h:128 * (h + 1)] = 1.0
    c["SELH"] = selh
    ps33 = np.zeros((33, 128), dtype=np.float32)
    ps33[0, 0:64] = 1.0
    ps33[32, 64:128] = 1.0
    c["PAIRSEL"] = ps33
    ex = np.zeros((32, 512), dtype=np.float32)
    for r in range(4):
        for p in range(128):
            ex[8 * r + p // 16, 128 * r + p] = 1.0
    c["EXPAND"] = ex
    lm = np.zeros((128, 128), dtype=np.float32)
    for p in range(128):
        lm[p, p:] = 1.0
    c["LMASK"] = lm
    bt = _bucket_table()
    oht = np.zeros((384, 4095), dtype=np.float32)
    oht[bt[::-1], np.arange(4095)] = 1.0
    c["OHT"] = oht
    return c


def prep_weights(iw):
    w = {}
    f32 = lambda a: np.ascontiguousarray(np.asarray(a), dtype=np.float32)
    w["qwT0"] = f32(np.asarray(iw["L0_attn_q_w"]).T)
    w["kwT0"] = f32(np.asarray(iw["L0_attn_k_w"]).T)
    w["vwT0"] = f32(np.asarray(iw["L0_attn_v_w"]).T)
    w["owT0"] = f32(np.asarray(iw["L0_attn_out_w"]).T)
    w["qb0"] = f32(np.asarray(iw["L0_attn_q_b"]).reshape(8, 128).T)
    w["kb0"] = f32(np.asarray(iw["L0_attn_k_b"]).reshape(8, 128).T)
    w["vb0"] = f32(np.asarray(iw["L0_attn_v_b"]).reshape(1, 1024))
    w["ob0"] = f32(np.asarray(iw["L0_attn_out_b"]).reshape(1, 1024))
    rel = np.zeros((384, 16), dtype=np.float32)
    rel[:320] = np.asarray(iw["L0_attn_rel_embed"])
    w["rel0"] = rel
    w["gc0"] = f32(np.asarray(iw["L0_attn_gru_const"]).reshape(1, 16))
    gbdT = np.zeros((1024, 128), dtype=np.float32)
    gw = np.asarray(iw["L0_attn_gru_w"], dtype=np.float32)
    for h in range(16):
        gbdT[64 * h:64 * (h + 1), 8 * h:8 * (h + 1)] = gw.T
    w["gbdT0"] = gbdT
    w["grub0"] = f32(np.tile(np.asarray(iw["L0_attn_gru_b"]), 16).reshape(1, 128))
    w["wrT0"] = f32(np.asarray(iw["L0_ffn_W_r"]).T)
    w["wkT0"] = f32(np.asarray(iw["L0_ffn_W_k"]).T)
    w["wvT0"] = f32(np.asarray(iw["L0_ffn_W_v"]).T)
    w["mur0"] = f32(np.asarray(iw["L0_ffn_mu_r"]).reshape(8, 128).T)
    w["muk0"] = f32(np.asarray(iw["L0_ffn_mu_k"]).reshape(8, 128).T)
    w["qwT1"] = f32(np.asarray(iw["L1_attn_w_q"]).reshape(1024, 1024).T)
    w["kwT1"] = f32(np.asarray(iw["L1_attn_w_k"]).reshape(1024, 1024).T)
    w["vwT1"] = f32(np.asarray(iw["L1_attn_w_v"]).reshape(1024, 1024).T)
    w["owT1"] = f32(np.asarray(iw["L1_attn_w_o"]).T)
    w["protT1"] = f32(np.asarray(iw["L1_ffn_prototypes"]).T)
    w["BT1"] = f32(np.asarray(iw["L1_ffn_B"]).reshape(512, 1024).T)
    w["AF1"] = f32(np.asarray(iw["L1_ffn_A"]).transpose(0, 2, 1).reshape(512, 1024))
    w["bias1"] = f32(np.asarray(iw["L1_ffn_bias"]))
    w["temp1"] = f32(np.asarray(iw["L1_ffn_temp"]).reshape(1, 1))
    return w


WSHAPES = dict(qwT0=[E, E], kwT0=[E, E], vwT0=[E, E], owT0=[E, E],
               qb0=[128, 8], kb0=[128, 8], vb0=[1, E], ob0=[1, E],
               rel0=[384, 16], gc0=[1, 16], gbdT0=[E, 128], grub0=[1, 128],
               wrT0=[E, E], wkT0=[E, E], wvT0=[E, E], mur0=[128, 8], muk0=[128, 8],
               qwT1=[E, E], kwT1=[E, E], vwT1=[E, E], owT1=[E, E],
               protT1=[E, 32], BT1=[E, 512], AF1=[512, E], bias1=[32, E], temp1=[1, 1],
               IDN=[128, 128], REVIDN=[128, 128], ONES1=[1, 128], ZCOL=[128, 1], HS=[128, 2],
               SELPNEG=[2, 256], SELH=[16, T], PAIRSEL=[33, 128],
               EXPAND=[32, 512], LMASK=[128, 128], OHT=[384, 4095])


class Sem:
    __slots__ = ("h", "count")

    def __init__(self, h):
        self.h = h
        self.count = 0


class Prog:
    def __init__(self, blocks=("l0a", "l0f", "l1a", "l1f"), n_batch=2):
        self.blocks = blocks
        self.n_batch = n_batch
        nc = bass.Bass("TRN2", target_bir_lowering=False, debug=False)
        self.nc = nc
        self.es = contextlib.ExitStack()
        self.ops = {e: [] for e in ("tensor", "vector", "scalar", "sync", "gpsimd")}
        self.sems = {}
        for name in ("T", "V", "S", "DY", "DG"):
            self.sems[name] = Sem(self.es.enter_context(nc.semaphore("sem_" + name)))
        self.T, self.V, self.S = self.sems["T"], self.sems["V"], self.sems["S"]
        self.DY, self.DG = self.sems["DY"], self.sems["DG"]

        dr = lambda n, sh: nc.dram_tensor(n, sh, F32, kind="ExternalInput").ap()
        self.x_in = dr("x_in", [n_batch, T, E])
        self.y_out = nc.dram_tensor("y_out", [n_batch, T, E], F32, kind="ExternalOutput").ap()
        self.d = {n: dr(n, sh) for n, sh in WSHAPES.items()}
        import os as _os
        self.dbg = None
        if _os.environ.get("K_DBG"):
            self.dbg = nc.dram_tensor("dbg", [128, 512], F32, kind="ExternalOutput").ap()
        self.pbx_dram = nc.dram_tensor("pbx_scr", [16, 4095], BF16, kind="Internal").ap()

        sb = lambda n, sh, dt: self.es.enter_context(nc.sbuf_tensor(n, sh, dt))
        self.resid = sb("resid", [128, NT * 1024], F32)
        self.xT = sb("xT", [128, NE * T], BF16)
        self.XA = sb("XA", [128, 8192], BF16)
        self.XB = sb("XB", [128, 8192], BF16)
        self.XC = sb("XC", [128, 1536], BF16)
        self.WT = sb("WT", [128, 12288], BF16)
        self.WT2 = sb("WT2", [128, 4096], BF16)
        self.pbrep = sb("pbrep", [128, 2 * PBW], BF16)
        self.idn = sb("idn", [128, 128], BF16)
        self.revidn = sb("revidn", [128, 128], BF16)
        self.ones1 = sb("ones1", [1, 128], BF16)
        self.zcol = sb("zcol", [128, 1], BF16)
        self.share32 = sb("share32", [32, T], BF16)
        self.gateT = self.share32[0:16, :]
        self.wT_sb = self.share32
        self.hs = sb("hs", [128, 2], BF16)
        self.selpneg = sb("selpneg", [2, 256], BF16)
        self.selh = sb("selh", [16, T], BF16)
        self.pairsel = sb("pairsel", [33, 128], F32)
        self.expand = sb("expand", [32, 512], BF16)
        self.lmask = sb("lmask", [128, 128], BF16)
        self.qb_sb = sb("qb_sb", [128, 8], F32)
        self.kb_sb = sb("kb_sb", [128, 8], F32)
        self.vo_rows = sb("vo_rows", [1, E], BF16)
        self.vb_sb = self.vo_rows[0:1, :]
        self.ob_row = sb("ob_row", [1, E], BF16)
        self.ob_sb = self.ob_row[0:1, :]
        self.grub_sb = sb("grub_sb", [1, 128], BF16)
        self.mur_sb = sb("mur_sb", [128, 8], F32)
        self.muk_sb = sb("muk_sb", [128, 8], F32)
        self.crep = sb("crep", [128, 16], F32)
        self.protT_sb = sb("protT_sb", [128, NE * 32], BF16)
        self.biasm_sb = sb("biasm_sb", [32, E], BF16)
        self.p2rep = sb("p2rep", [128, 32], F32)
        self.rtB = sb("rtB", [128, 1], F32)
        self.stat = sb("stat", [128, 64], F32)
        self.x2s = sb("x2s", [128, 16], F32)
        self.wfull = sb("wfull", [128, NT * 32], BF16)
        self.tpA = sb("tpA", [128, 32], F32)
        self.tpB = sb("tpB", [128, 32], F32)
        self.tpC = sb("tpC", [128, 32], F32)
        self.tpD = sb("tpD", [128, NT * 32], F32)
        self.sqs = self.tpD[:, :].bitcast(BF16)
        self.gp_o_view = self.tpD[:, :].bitcast(BF16)
        self.tpM = sb("tpM", [128, 8], F32)
        self.cm = sb("cm", [128, 2], F32)
        self.cmbf = sb("cmbf", [128, 2], BF16)
        self.c2 = sb("c2", [2, 8], F32)
        self.c2bf = sb("c2bf", [2, 2], BF16)
        self.bias_h = sb("bias_h", [128, 4], F32)
        self.zr = sb("zr", [33, 1024], F32)
        self.rrep = sb("rrep", [128, 512], F32)
        self.btL0 = sb("btL0", [2, 1], F32)
        self.smr = sb("smr", [1, 64], F32)
        self.smrb = sb("smrb", [1, 64], BF16)

        ps = lambda n, sh: self.es.enter_context(nc.psum_tensor(n, sh, F32))
        self.ps_sA = ps("ps_sA", [128, 1024])
        self.ps_sB = ps("ps_sB", [128, 1024])
        self.ps_se = self.ps_sA[:, 0:512]
        self.ps_so = self.ps_sA[:, 512:1024]
        self.ps_se2 = self.ps_sB[:, 0:512]
        self.ps_so2 = self.ps_sB[:, 512:1024]
        self.ps_sj = self.ps_se[:, :].tensor
        self.sj_view = None
        self.ps_ctx = ps("ps_ctx", [128, 512])
        self.ps_z = ps("ps_z", [33, 512])
        self.ps_r = ps("ps_r", [128, 512])
        self.ps_w = ps("ps_w", [128, 512])
        self.ps_reader = {}
        self.gp_war = {}
        self.wt_tick = {}
        self._seen_waits = {}
        self._norm_pending = None

    def op(self, eng, fn, inc=None, waits=(), inc_amt=1):
        wd = {}
        seen = self._seen_waits.setdefault(eng, {})
        for (s_, v_) in waits:
            if v_ > 0 and v_ > seen.get(id(s_), 0):
                wd[id(s_)] = (s_, max(v_, wd.get(id(s_), (s_, 0))[1]))
        for (s_, v_) in wd.values():
            seen[id(s_)] = v_
        waits = tuple(wd.values())
        rec = {"fn": fn, "waits": waits, "incs": []}
        if inc is not None:
            inc.count += inc_amt
            rec["incs"].append((inc.h, inc_amt))
        self.ops[eng].append(rec)

    def dma(self, queue, out, in_, waits=(), **kw):
        sem = self.DY if queue == "sync" else self.DG
        self.op(queue, lambda e: e.dma_start(out=out, in_=in_, **kw), inc=sem,
                waits=waits, inc_amt=16)
        return (sem, sem.count)

    def mm(self, out, lhsT, rhs, start, stop, waits=(), inc=None, tp=None, auto_war=True):
        kw = {"skip_group_check": True}
        if tp is not None:
            kw["tile_position"] = tp
        if start and auto_war:
            war = self.ps_reader.get(out.tensor.name)
            if war is not None:
                waits = list(waits) + [war]
        self.op("tensor", lambda e: e.matmul(out, lhsT, rhs, start=start, stop=stop, **kw),
                inc=inc, waits=waits)

    def ps_war(self, pst):
        w = self.ps_reader.get(pst.tensor.name if hasattr(pst, "tensor") else pst[:, :].tensor.name)
        return [w] if w is not None else []

    def mark_ps(self, pst, eng="V"):
        sem = self.V if eng == "V" else self.S
        self.ps_reader[pst.tensor.name if hasattr(pst, "tensor") else pst[:, :].tensor.name] = (sem, sem.count)

    def v(self, fn, waits=()):
        self.op("vector", fn, inc=self.V, waits=waits)

    def vq(self, fn, waits=()):
        self.op("vector", fn, inc=self.V, waits=list(waits) + [(self.V, self.V.count)])

    def s(self, fn, waits=()):
        self.op("scalar", fn, inc=self.S, waits=waits)

    def tmark(self):
        rec = self.ops["tensor"][-1]
        for (h, amt) in rec["incs"]:
            if h is self.T.h:
                return (self.T, self.T.count)
        self.T.count += 1
        rec["incs"].append((self.T.h, 1))
        return (self.T, self.T.count)

    def tnow(self):
        return (self.T, self.T.count)

    def vnow(self):
        return (self.V, self.V.count)

    def snow(self):
        return (self.S, self.S.count)

    def finalize(self):
        nc = self.nc

        def replay(e, recs):
            for rec in recs:
                for (s_, v_) in rec["waits"]:
                    e.wait_ge(s_.h, v_)
                inst = rec["fn"](e)
                for (h, amt) in rec["incs"]:
                    inst = inst.then_inc(h, amt)

        ops = self.ops
        with nc.Block() as block:
            @block.tensor
            def _(e):
                replay(e, ops["tensor"])

            @block.vector
            def _(e):
                replay(e, ops["vector"])

            @block.scalar
            def _(e):
                replay(e, ops["scalar"])

            @block.sync
            def _(e):
                replay(e, ops["sync"])

            @block.gpsimd
            def _(e):
                replay(e, ops["gpsimd"])
        self.es.close()
        return nc

    def emit_clear_all(self):
        big = [self.resid, self.xT, self.XA, self.XB, self.XC, self.WT, self.WT2,
               self.pbrep, self.share32, None]
        for t in [self.resid, self.xT, self.XA, self.XB, self.XC, self.WT, self.WT2,
                  self.pbrep, self.share32, self.wfull, self.tpD, self.stat, self.x2s,
                  self.zr, self.rrep, self.cm, self.c2, self.bias_h, self.btL0,
                  self.tpA, self.tpB, self.tpC, self.tpM, self.smr, None]:
            if t is None:
                continue
            self.v(lambda e, t=t: e.memset(t[:, :], 0.0))
        for p in (self.ps_se, self.ps_se2, self.ps_so, self.ps_so2, self.ps_ctx,
                  self.ps_r, self.ps_w):
            self.v(lambda e, p=p: e.memset(p[:, :], 0.0))
        self.v(lambda e: e.memset(self.ps_z[:, :], 0.0))
        self.clear_done = self.vnow()

    def emit_startup(self):
        d = self.d
        cw0 = [self.clear_done]
        self.op("gpsimd", lambda e: e.nop(), waits=cw0)
        self.op("sync", lambda e: e.nop(), waits=cw0)
        for src, dst in [(d["IDN"], self.idn), (d["REVIDN"], self.revidn), (d["ONES1"], self.ones1),
                         (d["ZCOL"], self.zcol), (d["HS"], self.hs),
                         (d["SELPNEG"], self.selpneg), (d["SELH"], self.selh),
                         (d["EXPAND"], self.expand), (d["LMASK"], self.lmask),
                         (d["vb0"], self.vo_rows[0:1, :]), (d["ob0"], self.ob_row[0:1, :]),
                         (d["grub0"], self.grub_sb), (d["bias1"], self.biasm_sb)]:
            self.dma("gpsimd", dst[:, :], src[:, :])
        for n, dst in [("PAIRSEL", self.pairsel), ("qb0", self.qb_sb), ("kb0", self.kb_sb),
                       ("mur0", self.mur_sb), ("muk0", self.muk_sb)]:
            self.dma("sync", dst[:, :], d[n][:, :])
        self.v(lambda e: e.memset(self.zr[:, :], 1.0))
        self.v(lambda e: e.memset(self.rrep[:, :], 1.0))
        for m in range(NE):
            tk_pt = self.dma("gpsimd", self.protT_sb[:, 32 * m:32 * (m + 1)],
                             d["protT1"].rearrange("(m p) n -> m p n", p=128)[m])

        if "l0a" in self.blocks:
            t_oht = []
            oht_dst = [self.XA[:, 0:4095], self.XA[:, 4096:8191], self.XB[:, 0:4095]]
            for kt in range(3):
                t_oht.append(self.dma("gpsimd", oht_dst[kt],
                                      d["OHT"].rearrange("(k p) j -> k p j", p=128)[kt]))
            t_rel = []
            for kt in range(3):
                t_rel.append(self.dma("gpsimd", self.WT[:, 16 * kt:16 * (kt + 1)],
                                      d["rel0"].rearrange("(k p) j -> k p j", p=128)[kt]))
            w0 = [t_oht[-1], t_rel[-1]]
            for ch in range(8):
                n0 = 512 * ch
                n1 = min(4095, n0 + 512)
                for kt in range(3):
                    src = [self.XA[:, n0:n1], self.XA[:, 4096 + n0:4096 + n1],
                           self.XB[:, n0:n1]][kt]
                    self.mm(self.ps_w[0:16, 0:n1 - n0], self.WT[:, 16 * kt:16 * (kt + 1)],
                            src, start=(kt == 0), stop=(kt == 2),
                            waits=w0 if ch == 0 else ())
                tm = self.tmark()
                self.v(lambda e, n0=n0, n1=n1: e.tensor_copy(
                    self.XB[0:16, 4096 + n0:4096 + n1], self.ps_w[0:16, 0:n1 - n0]), waits=[tm])
                self.mark_ps(self.ps_w)
            vv = self.vnow()
            self.pbx_ready = self.dma("sync", self.pbx_dram[:, :], self.XB[0:16, 4096:4096 + 4095],
                                      waits=[vv])
            self.vq(lambda e: e.reduce_max(self.tpA[0:16, 0:1], self.XB[0:16, 4096:4096 + 4095],
                                          axis=AX.X, apply_absolute_value=True), waits=[vv])
            self.vq(lambda e: e.tensor_copy(self.cmbf[0:16, 0:1], self.tpA[0:16, 0:1]))
            vv = self.vnow()
            self.mm(self.ps_r[0:1, 0:128], self.cmbf[0:16, 0:1], self.idn[0:16, :], True, True,
                    waits=[vv])
            tm = self.tmark()
            self.vq(lambda e: e.reduce_max(self.smr[0:1, 0:1], self.ps_r[0:1, 0:16], axis=AX.X),
                   waits=[tm])
            self.mark_ps(self.ps_r)
            self.vq(lambda e: e.tensor_copy(self.smrb[0:1, 0:1], self.smr[0:1, 0:1]))
            vv = self.vnow()
            self.mm(self.ps_r[0:2, 0:1], self.ones1[0:1, 0:2], self.smrb[0:1, 0:1], True, True,
                    waits=[vv])
            tmp1 = self.tmark()
            tgc = self.dma("gpsimd", self.smrb[0:1, 16:32], d["gc0"][:, :])
            self.mm(self.ps_w[:, 0:16], self.ones1[:, :], self.smrb[0:1, 16:32], True, True,
                    waits=[tgc])
            tmp2 = self.tmark()
            self.vq(lambda e: e.tensor_copy(self.c2[:, 6:7], self.ps_r[0:2, 0:1]), waits=[tmp1])
            self.mark_ps(self.ps_r)
            self.vq(lambda e: e.tensor_copy(self.crep[:, :], self.ps_w[:, 0:16]), waits=[tmp2])
            self.mark_ps(self.ps_w)
            self.vq(lambda e: e.reduce_max(self.c2[0:2, 7:8], self.crep[0:2, :], axis=AX.X,
                                          apply_absolute_value=True))
            self.vq(lambda e: e.tensor_scalar_add(self.c2[:, 7:8], self.c2[:, 7:8], 3.0))
            self.vq(lambda e: e.tensor_mul(self.btL0[:, :], self.c2[:, 7:8], self.c2[:, 6:7]))
            self.gbd_ready = None

        if "l1f" in self.blocks:
            tt = self.dma("sync", self.smr[0:1, 2:3], d["temp1"][:, :])
            self.s(lambda e: e.activation(self.smr[0:1, 3:4], self.smr[0:1, 2:3], AF.Abs),
                   waits=[tt])
            sv = self.snow()
            self.vq(lambda e: e.tensor_scalar_max(self.smr[0:1, 4:5], self.smr[0:1, 3:4], 0.1),
                   waits=[sv])
            self.vq(lambda e: e.reciprocal(self.smr[0:1, 5:6], self.smr[0:1, 4:5]))
            self.vq(lambda e: e.tensor_scalar_mul(self.smr[0:1, 6:7], self.smr[0:1, 5:6], -1.0))
            self.vq(lambda e: e.tensor_copy(self.smrb[0:1, 2:3], self.smr[0:1, 6:7]))
            vv = self.vnow()
            self.mm(self.ps_r[:, 16:17], self.ones1[:, :], self.smrb[0:1, 2:3], True, True,
                    waits=[vv])
            tm = self.tmark()
            self.vq(lambda e: e.tensor_copy(self.rtB[:, :], self.ps_r[:, 16:17]), waits=[tm])
            self.mark_ps(self.ps_r)
            for m in range(NE):
                self.vq(lambda e, m=m: e.tensor_tensor(
                    self.XC[:, 32 * m:32 * (m + 1)], self.protT_sb[:, 32 * m:32 * (m + 1)],
                    self.protT_sb[:, 32 * m:32 * (m + 1)], ALU.mult),
                    waits=[tk_pt] if m == 0 else ())
            vv = self.vnow()
            for m in range(NE):
                self.mm(self.ps_w[0:1, 32:64], self.zcol[:, :], self.XC[:, 32 * m:32 * (m + 1)],
                        start=(m == 0), stop=(m == NE - 1), waits=[vv] if m == 0 else ())
            tm = self.tmark()
            self.vq(lambda e: e.tensor_copy(self.smr[0:1, 32:64], self.ps_w[0:1, 32:64]), waits=[tm])
            self.mark_ps(self.ps_w)
            self.vq(lambda e: e.tensor_copy(self.smrb[0:1, 32:64], self.smr[0:1, 32:64]))
            vv = self.vnow()
            self.mm(self.ps_w[:, 32:64], self.ones1[:, :], self.smrb[0:1, 32:64], True, True,
                    waits=[vv])
            tm = self.tmark()
            self.vq(lambda e: e.tensor_copy(self.p2rep[:, :], self.ps_w[:, 32:64]), waits=[tm])
            self.mark_ps(self.ps_w)

    def emit_load_resid(self, bi):
        for j in range(NT):
            tk = self.dma("sync", self.resid[:, 1024 * j:1024 * (j + 1)],
                          self.x_in[bi].rearrange("(j p) e -> j p e", p=128)[j])
        self.resid_ready = tk

    def emit_make_xT(self, extra_waits=()):
        cw = list(extra_waits) + [self.tnow(), (self.DG, self.DG.count),
                                  (self.DY, self.DY.count)]
        for half in range(2):
            cwh = cw if half == 0 else [self.tnow()]
            for jj in range(8):
                j = 8 * half + jj
                dst = self.XA[:, 1024 * jj:1024 * (jj + 1)]
                src = self.resid[:, 1024 * j:1024 * (j + 1)]
                if jj % 2 == 0:
                    self.v(lambda e, dst=dst, src=src: e.tensor_copy(dst, src),
                           waits=cwh if jj == 0 else ())
                else:
                    self.s(lambda e, dst=dst, src=src: e.copy(dst, src),
                           waits=cwh if jj == 1 else ())
            vw, sw = self.vnow(), self.snow()
            for jj in range(8):
                j = 8 * half + jj
                src = self.XA[:, 1024 * jj:1024 * (jj + 1)]
                pair_ps = ((self.ps_w, self.ps_r), (self.ps_se, self.ps_so))[j % 2]
                for g, pst in ((0, pair_ps[0]), (1, pair_ps[1])):
                    for m4 in range(4):
                        m = 4 * g + m4
                        self.mm(pst[:, 128 * m4:128 * (m4 + 1)], src[:, 128 * m:128 * (m + 1)],
                                self.idn[:, :], True, True,
                                waits=[vw, sw] if (jj == 0 and g == 0 and m4 == 0) else ())
                tm = self.tmark()
                for g, pst in ((0, pair_ps[0]), (1, pair_ps[1])):
                    xt_dst = self.xT[:, :].rearrange("p (m t) -> p m t", t=T)[:, 4 * g:4 * (g + 1), 128 * j:128 * (j + 1)]
                    self.v(lambda e, xt_dst=xt_dst, pst=pst: e.tensor_copy(
                        xt_dst, pst[:, :].rearrange("p (m c) -> p m c", c=128)),
                        waits=[tm])
                    self.mark_ps(pst)
        self.xT_ready = self.vnow()

    def emit_store_out(self, bi, waits=()):
        for j in range(NT):
            tk = self.dma("sync", self.y_out[bi].rearrange("(j p) e -> j p e", p=128)[j],
                          self.resid[:, 1024 * j:1024 * (j + 1)],
                          waits=waits if j == 0 else ())
        self.out_done = tk

    def tile_addz(self, j, psA, psB, waits=(), ln=True):
        rj = self.resid[:, 1024 * j:1024 * (j + 1)]
        accA = self.stat[:, j:j + 1] if ln else None
        accB = self.stat[:, 16 + j:17 + j] if ln else None
        self.v(lambda e: e.scalar_tensor_tensor(rj[:, 0:512], psA[:, :], 1.0, rj[:, 0:512],
                                                ALU.mult, ALU.add, accum_out=accA),
               waits=waits)
        self.mark_ps(psA)
        self.v(lambda e: e.scalar_tensor_tensor(rj[:, 512:1024], psB[:, :], 1.0, rj[:, 512:1024],
                                                ALU.mult, ALU.add, accum_out=accB))
        self.mark_ps(psB)
        if ln:
            vw = self.vnow()
            self.s(lambda e: e.activation(self.sqs[:, :], rj, AF.Square,
                                          accum_out=self.stat[:, 32 + j:33 + j]), waits=[vw])

    def emit_ln_finish(self, need_xT=True, for_l1f=False):
        st = self.stat
        sw = self.snow()
        import os as _os2
        if self.dbg is not None and _os2.environ.get("K_DBG") == "stat" and not getattr(self, "_dbg_done", False):
            self._dbg_done = True
            tkd = self.dma("sync", self.dbg[:, 0:64], st[:, :], waits=[sw, self.vnow()])
            self.vq(lambda e: e.memset(self.smr[0:1, 60:61], 0.0), waits=[tkd])
        self.vq(lambda e: e.tensor_add(st[:, 0:16], st[:, 0:16], st[:, 16:32]), waits=[sw])
        self.vq(lambda e: e.tensor_scalar_mul(st[:, 48:64], st[:, 0:16], 1.0 / 1024.0))
        self.vq(lambda e: e.tensor_scalar_mul(st[:, 0:16], st[:, 32:48], 1.0 / 1024.0))
        self.vq(lambda e: e.tensor_mul(st[:, 16:32], st[:, 48:64], st[:, 48:64]))
        self.vq(lambda e: e.tensor_sub(st[:, 0:16], st[:, 0:16], st[:, 16:32]))
        self.vq(lambda e: e.tensor_scalar_add(st[:, 16:32], st[:, 0:16], EPS))
        vw = self.vnow()
        self.s(lambda e: e.activation(st[:, 32:48], st[:, 16:32], AF.Sqrt), waits=[vw])
        sw = self.snow()
        self.vq(lambda e: e.reciprocal(st[:, 16:32], st[:, 32:48]), waits=[sw])
        if for_l1f:
            self.vq(lambda e: e.tensor_mul(self.x2s[:, :], st[:, 0:16], st[:, 16:32]))
            self.vq(lambda e: e.tensor_mul(self.x2s[:, :], self.x2s[:, :], st[:, 16:32]))
            self.vq(lambda e: e.tensor_scalar_mul(self.x2s[:, :], self.x2s[:, :], 1024.0))
            self.x2_done = self.vnow()
        self.vq(lambda e: e.tensor_scalar_mul(st[:, 32:48], st[:, 48:64], -1.0))
        for j in range(NT):
            rj = self.resid[:, 1024 * j:1024 * (j + 1)]
            self.vq(lambda e, j=j, rj=rj: e.tensor_scalar(
                rj, rj, st[:, 32 + j:33 + j], st[:, 16 + j:17 + j], ALU.add, ALU.mult))
        self.ln_done = self.vnow()
        self.v(lambda e: e.memset(st[:, 0:48], 0.0), waits=[self.ln_done])
        if need_xT:
            self.emit_make_xT(extra_waits=[self.ln_done])

    def emit_attn(self, layer):
        import os as _os
        astage = int(_os.environ.get("K_ATT_STAGE", "4"))
        L0 = (layer == 0)
        d = self.d
        qwT = d["qwT0"] if L0 else d["qwT1"]
        kwT = d["kwT0"] if L0 else d["kwT1"]
        vwT = d["vwT0"] if L0 else d["vwT1"]
        owT = d["owT0"] if L0 else d["owT1"]

        if L0:
            for m in range(NE):
                self.gbd_ready = self.dma("gpsimd", self.WT[:, 128 * m:128 * (m + 1)],
                                          self.d["gbdT0"].rearrange("(m p) n -> m p n", p=128)[m],
                                          waits=[self.tnow()] if m == 0 else ())
            self._emit_gate()

        for grp in range(2):
            ow_tks = []
            for kp in range(4):
                pg = 4 * grp + kp
                ow_tks.append(self.dma(
                    "gpsimd", self.WT[:, 8192 + 1024 * kp:8192 + 1024 * (kp + 1)],
                    owT.rearrange("(k p) n -> k p n", p=128)[pg],
                    waits=[self.tnow()] if (grp > 0) else ()))
            for kp in range(4):
                pair = 4 * grp + kp
                self._emit_attn_pair(layer, pair, kp, qwT, kwT, vwT, astage)
            if astage >= 4:
                self._emit_outproj(grp, last=(grp == 1), ow_wait=[ow_tks[-1]])
        if astage < 4:
            self.ln_done = self.vnow()
            return
        need_xT = True if L0 else ("l1f" in self.blocks)
        self.emit_ln_finish(need_xT=need_xT, for_l1f=(not L0))

    def _emit_gate(self):
        gw = [self.gbd_ready, self.xT_ready]
        for j in range(NT):
            for m in range(NE):
                self.mm(self.ps_w[:, 0:128],
                        self.xT[:, T * m + 128 * j: T * m + 128 * (j + 1)],
                        self.WT[:, 128 * m:128 * (m + 1)],
                        start=(m == 0), stop=False, waits=gw if (j == 0 and m == 0) else ())
            self.mm(self.ps_w[:, 0:128], self.ones1[:, :], self.grub_sb[:, :],
                    start=False, stop=True)
            tm = self.tmark()
            self.v(lambda e: e.tensor_reduce(
                self.tpA[:, 0:32], self.ps_w[:, 0:128].rearrange("p (g i) -> p g i", i=4),
                axis=AX.X, op=ALU.add), waits=[tm])
            self.mark_ps(self.ps_w)
            vw = self.vnow()
            self.s(lambda e: e.activation(self.tpB[:, 0:32], self.tpA[:, 0:32], AF.Sigmoid),
                   waits=[vw])
            sw = self.snow()
            gb = self.tpB[:, 0:32].rearrange("p (h two) -> p h two", two=2)
            self.vq(lambda e: e.tensor_tensor(self.tpC[:, 0:16], gb[:, :, 1:2].squeeze(2),
                                             self.crep[:, :], ALU.mult), waits=[sw])
            self.vq(lambda e: e.tensor_scalar_add(self.tpC[:, 0:16], self.tpC[:, 0:16], -1.0))
            self.vq(lambda e: e.tensor_tensor(self.tpC[:, 16:32], gb[:, :, 0:1].squeeze(2),
                                             self.tpC[:, 0:16], ALU.mult))
            self.vq(lambda e: e.tensor_scalar_add(self.XC[:, 0:16], self.tpC[:, 16:32], 2.0),
                   waits=[self.tnow()])
            vw = self.vnow()
            self.mm(self.ps_r[0:16, 0:128], self.XC[:, 0:16], self.idn[:, :], True, True,
                    waits=[vw])
            tm = self.tmark()
            self.v(lambda e, j=j: e.tensor_copy(self.gateT[:, 128 * j:128 * (j + 1)],
                                                self.ps_r[0:16, 0:128]), waits=[tm])
            self.mark_ps(self.ps_r)
        self.gate_ready = self.vnow()

    def _emit_attn_pair(self, layer, pair, slot, qwT, kwT, vwT, astage=4):
        if self._norm_pending is not None:
            self._norm_pending()
            self._norm_pending = None
        L0 = (layer == 0)
        XB = self.XB
        sA = 3072 * (slot % 2)
        qof, kof, vof = sA, sA + 1024, sA + 2048
        war_g = [self.wt_tick.get(slot % 2, (self.T, 0))]
        wsl = lambda src: src.rearrange("p (m n) -> p m n", n=128)
        tq = self.dma("gpsimd", wsl(self.WT[:, qof:qof + 1024]),
                      AP(qwT.tensor, 128 * pair, [[1024, 128], [131072, NE], [1, 128]]),
                      waits=war_g)
        tk = self.dma("gpsimd", wsl(self.WT[:, kof:kof + 1024]),
                      AP(kwT.tensor, 128 * pair, [[1024, 128], [131072, NE], [1, 128]]))
        tv = self.dma("gpsimd", wsl(self.WT[:, vof:vof + 1024]),
                      AP(vwT.tensor, 128 * pair, [[1024, 128], [131072, NE], [1, 128]]))
        for (wof, dst0, tkw, bias, do_scale) in ((qof, 0, tq, self.qb_sb, L0),
                                                 (kof, 2048, tk, self.kb_sb, False)):
            for ch in range(4):
                pst = self.ps_se if ch % 2 == 0 else self.ps_se2
                for m in range(NE):
                    self.mm(pst[:, :], self.WT[:, wof + 128 * m:wof + 128 * (m + 1)],
                            self.xT[:, T * m + 512 * ch:T * m + 512 * (ch + 1)],
                            start=(m == 0), stop=(m == NE - 1),
                            waits=[tkw, self.xT_ready, self.snow(), self.vnow()] if (ch == 0 and m == 0) else ())
                tm = self.tmark()
                dst = XB[:, dst0 + 512 * ch:dst0 + 512 * (ch + 1)]
                if L0 and do_scale:
                    self.v(lambda e, pst=pst, dst=dst: e.tensor_scalar(
                        dst, pst[:, :], bias[:, pair:pair + 1], 0.125, ALU.add, ALU.mult),
                        waits=[tm])
                elif L0:
                    self.v(lambda e, pst=pst, dst=dst: e.tensor_scalar(
                        dst, pst[:, :], bias[:, pair:pair + 1], None, ALU.add), waits=[tm])
                else:
                    self.v(lambda e, pst=pst, dst=dst: e.tensor_copy(dst, pst[:, :]),
                           waits=[tm])
                self.mark_ps(pst)
        self.v(lambda e: e.reduce_max(self.cm[:, 0:1], XB[:, 0:2048], axis=AX.X,
                                      apply_absolute_value=True))
        self.v(lambda e: e.reduce_max(self.cm[:, 1:2], XB[:, 2048:4096], axis=AX.X,
                                      apply_absolute_value=True))
        for j in range(NT):
            pst = self.ps_so if j % 2 == 0 else self.ps_so2
            for m in range(NE):
                self.mm(pst[:, 0:128], self.xT[:, T * m + 128 * j:T * m + 128 * (j + 1)],
                        self.WT[:, vof + 128 * m:vof + 128 * (m + 1)],
                        start=(m == 0), stop=((not L0) and m == NE - 1),
                        waits=[tv] if (j == 0 and m == 0) else ())
            if L0:
                self.mm(pst[:, 0:128], self.ones1[:, :],
                        self.vb_sb[:, 128 * pair:128 * (pair + 1)], start=False, stop=True)
            tm = self.tmark()
            self.v(lambda e, pst=pst, j=j: e.tensor_copy(
                XB[:, 4096 + 128 * j:4096 + 128 * (j + 1)], pst[:, 0:128]), waits=[tm])
            self.mark_ps(pst)
        proj_done = self.vnow()
        self.wt_tick[slot % 2] = self.tnow()
        if astage < 2:
            return

        self.vq(lambda e: e.tensor_mul(self.tpA[:, 0:2], self.cm[:, :], self.cm[:, :]))
        self.vq(lambda e: e.tensor_copy(self.cmbf[:, :], self.tpA[:, 0:2]))
        vw = self.vnow()
        self.mm(self.ps_r[0:2, 0:1], self.hs[:, :], self.cmbf[:, 0:1], True, True, waits=[vw])
        self.mm(self.ps_r[0:2, 1:2], self.hs[:, :], self.cmbf[:, 1:2], True, True)
        tm = self.tmark()
        self.vq(lambda e: e.tensor_copy(self.c2[:, 0:2], self.ps_r[0:2, 0:2]), waits=[tm])
        self.mark_ps(self.ps_r)
        if L0:
            self.vq(lambda e: e.tensor_scalar_mul(self.c2[:, 2:3], self.c2[:, 0:1], 4.0))
            self.vq(lambda e: e.tensor_scalar(self.c2[:, 3:4], self.c2[:, 1:2], 0.0625,
                                              None, ALU.mult))
            self.vq(lambda e: e.tensor_add(self.c2[:, 3:4], self.c2[:, 2:3], self.c2[:, 3:4]))
            self.vq(lambda e: e.tensor_add(self.c2[:, 4:5], self.c2[:, 3:4], self.btL0[:, :]))
        else:
            self.vq(lambda e: e.tensor_add(self.c2[:, 2:3], self.c2[:, 0:1], self.c2[:, 1:2]))
            self.vq(lambda e: e.tensor_scalar_mul(self.c2[:, 4:5], self.c2[:, 2:3], 0.0625))
        self.vq(lambda e: e.tensor_copy(self.c2bf[:, 0:1], self.c2[:, 4:5]))
        vw = self.vnow()
        self.mm(self.ps_r[0:1, 0:128], self.c2bf[:, 0:1], self.idn[0:2, :], True, True,
                waits=[vw] + self.ps_war(self.ps_r))
        tm = self.tmark()
        self.vq(lambda e: e.reduce_max(self.smr[0:1, 10:11], self.ps_r[0:1, 0:2], axis=AX.X),
                waits=[tm])
        self.mark_ps(self.ps_r)
        self.vq(lambda e: e.tensor_scalar_mul(self.smr[0:1, 11:12], self.smr[0:1, 10:11], -1.0))
        self.vq(lambda e: e.tensor_copy(self.smrb[0:1, 11:12], self.smr[0:1, 11:12]))
        vw = self.vnow()
        self.mm(self.ps_r[:, 2:3], self.ones1[:, :], self.smrb[0:1, 11:12], True, True,
                waits=[vw])
        tm = self.tmark()
        bslot = 2 * (pair % 2)
        self.vq(lambda e, bslot=bslot: e.tensor_copy(self.bias_h[:, bslot:bslot + 1],
                                                    self.ps_r[:, 2:3]),
               waits=[tm, self.snow()])
        self.mark_ps(self.ps_r)
        bias_ready = self.vnow()
        import os as _os4
        if self.dbg is not None and _os4.environ.get("K_DBG") == "cb" and pair == 0 and not getattr(self, "_dbg_done", False):
            self._dbg_done = True
            tkd = self.dma("sync", self.dbg[:, 0:4], self.bias_h[:, :], waits=[bias_ready])
            tkd = self.dma("sync", self.dbg[0:2, 8:16], self.c2[:, :])
            tkd = self.dma("sync", self.dbg[0:2, 16:17], self.btL0[:, :], allow_slow_non_contiguous=True)
            tkd = self.dma("sync", self.dbg[:, 32:48], self.crep[:, :])
            tkd = self.dma("sync", self.dbg[:, 48:50], self.cm[:, :])
            self.vq(lambda e: e.memset(self.smr[0:1, 60:61], 0.0), waits=[tkd])
        if astage < 3:
            return

        if L0:
            pb_e = self.dma("sync", self.pbrep[:, 0:PBW],
                            AP(self.pbx_dram.tensor, 4095 * (2 * pair),
                               [[1, 128], [1, PBW]]),
                            waits=[self.pbx_ready, getattr(self, "pb_war", (self.V, 0))])
            pb_o = self.dma("sync", self.pbrep[:, PBW:2 * PBW],
                            AP(self.pbx_dram.tensor, 4095 * (2 * pair + 1),
                               [[1, 128], [1, PBW]]))

        if L0:
            for c4 in range(4):
                for par in range(2):
                    self.mm(self.ps_w[:, :],
                            self.selh[:, 128 * (2 * pair + par):128 * (2 * pair + par) + 128],
                            self.gateT[:, 512 * c4:512 * (c4 + 1)], True, True,
                            waits=[self.gate_ready] if (c4 == 0 and par == 0) else ())
                    tm = self.tmark()
                    self.v(lambda e, c4=c4, par=par: e.tensor_copy(
                        self.WT2[:, 2048 * par + 512 * c4:2048 * par + 512 * (c4 + 1)],
                        self.ps_w[:, :]), waits=[tm])
                    self.mark_ps(self.ps_w)
            g_ready = self.vnow()
        ctx_base = 2048 * slot
        for c in range(4):
            nlive = 16 if L0 else 4 * c + 4
            ctx_ps = self.ps_ctx if c % 2 == 0 else self.ps_w
            if L0:
                pass
            avq = []
            for si in range(nlive):
                s0 = 128 * si
                buf = si % 2
                sj = self.ps_sA if buf == 0 else self.ps_sB
                ps_e = sj[:, 0:512]
                ps_o = sj[:, 512:1024]
                exp_e = XB[:, 6144 + 1024 * buf:6144 + 1024 * buf + 512]
                exp_o = XB[:, 6144 + 1024 * buf + 512:6144 + 1024 * (buf + 1)]
                w_first = [bias_ready] + ([g_ready] if L0 else [])
                if L0:
                    pcol = 1920 - s0 + 512 * c
                    gpw = [self.gp_war.get(buf, (self.T, 0))]
                    if c == 0 and si == 0:
                        gpw += [pb_e, pb_o]
                    gpe = self.XC[:, 512 * buf:512 * (buf + 1)]
                    gpo = self.gp_o_view[:, 512 * buf:512 * (buf + 1)]
                    self.v(lambda e, pcol=pcol, c=c, gpe=gpe: e.tensor_tensor(
                        gpe, self.WT2[:, 512 * c:512 * (c + 1)],
                        self.pbrep[:, pcol:pcol + 512], ALU.mult),
                        waits=gpw)
                    self.v(lambda e, pcol=pcol, c=c, gpo=gpo: e.tensor_tensor(
                        gpo, self.WT2[:, 2048 + 512 * c:2048 + 512 * (c + 1)],
                        self.pbrep[:, PBW + pcol:PBW + pcol + 512], ALU.mult))
                    gp_done = self.vnow()
                wqk = (list(w_first) if si <= 1 else []) + self.ps_war(ps_o)
                self.mm(ps_e[:, :], XB[0:64, 2048 + 128 * si:2048 + 128 * (si + 1)],
                        XB[0:64, 512 * c:512 * (c + 1)],
                        start=True, stop=not L0, waits=wqk)
                self.mm(ps_o[:, :], XB[64:128, 2048 + 128 * si:2048 + 128 * (si + 1)],
                        XB[64:128, 512 * c:512 * (c + 1)],
                        start=True, stop=not L0, auto_war=False)
                if L0:
                    self.mm(ps_e[:, :], self.revidn[:, :],
                            self.XC[:, 512 * buf:512 * (buf + 1)],
                            start=False, stop=True, waits=[gp_done], auto_war=False)
                    self.mm(ps_o[:, :], self.revidn[:, :],
                            self.gp_o_view[:, 512 * buf:512 * (buf + 1)],
                            start=False, stop=True, auto_war=False)
                    self.gp_war[buf] = self.tnow()
                    add_done = None
                if avq:
                    avq.pop(0)()
                if si == 2 and self._norm_pending is not None:
                    self._norm_pending()
                    self._norm_pending = None
                tm = self.tmark()
                exp_pair = XB[:, 6144 + 1024 * buf:6144 + 1024 * (buf + 1)]
                wexp = [tm]
                self.s(lambda e, sj=sj, exp_pair=exp_pair, bs=bslot: e.activation(
                    exp_pair, sj[:, :], AF.Exp, bias=self.bias_h[:, bs:bs + 1],
                    scale=1.0 if L0 else 0.125), waits=wexp)
                self.mark_ps(sj, "S")
                if (not L0) and si >= 4 * c:
                    k = si - 4 * c
                    sw = self.snow()
                    if k > 0:
                        self.v(lambda e, exp_e=exp_e, k=k: e.memset(exp_e[:, 0:128 * k], 0.0),
                               waits=[sw])
                        self.v(lambda e, exp_o=exp_o, k=k: e.memset(exp_o[:, 0:128 * k], 0.0))
                        sw = ()
                    self.v(lambda e, exp_e=exp_e, k=k: e.tensor_tensor(
                        exp_e[:, 128 * k:128 * (k + 1)], exp_e[:, 128 * k:128 * (k + 1)],
                        self.lmask[:, :], ALU.mult), waits=[sw] if sw else ())
                    self.v(lambda e, exp_o=exp_o, k=k: e.tensor_tensor(
                        exp_o[:, 128 * k:128 * (k + 1)], exp_o[:, 128 * k:128 * (k + 1)],
                        self.lmask[:, :], ALU.mult))
                    mask_done = [self.vnow()]
                else:
                    mask_done = [self.snow()]

                def emit_av(si=si, exp_e=exp_e, exp_o=exp_o, mask_done=mask_done,
                            first=(si == 0), last=(si == nlive - 1), ctx_ps=ctx_ps):
                    wav = list(mask_done)
                    if first:
                        wav += self.ps_war(ctx_ps) + self.ps_war(self.ps_z)
                    self.mm(ctx_ps[0:64, :],
                            XB[:, 4096 + 128 * si:4096 + 128 * si + 64],
                            exp_e, start=first, stop=last, tp=(0, 0), waits=wav,
                            auto_war=False)
                    self.mm(ctx_ps[64:128, :],
                            XB[:, 4096 + 128 * si + 64:4096 + 128 * (si + 1)],
                            exp_o, start=first, stop=last, tp=(0, 64), auto_war=False)
                    self.mm(self.ps_z[0:1, :], self.zcol[:, :], exp_e,
                            start=first, stop=last, tp=(0, 0), auto_war=False)
                    self.mm(self.ps_z[32:33, :], self.zcol[:, :], exp_o,
                            start=first, stop=last, tp=(0, 32), auto_war=False)
                avq.append(emit_av)
            while avq:
                avq.pop(0)()
            if L0:
                self.pb_war = self.vnow()
            tm = self.tmark()
            zc = 512 * (c % 2)
            self.v(lambda e, zc=zc: e.tensor_copy(self.zr[0:1, zc:zc + 512],
                                                  self.ps_z[0:1, :]), waits=[tm])
            self.v(lambda e, zc=zc: e.tensor_copy(self.zr[32:33, zc:zc + 512],
                                                  self.ps_z[32:33, :]))
            self.mark_ps(self.ps_z)

            def emit_norm(c=c, zc=zc, ctx_ps=ctx_ps):
                self.vq(lambda e: e.tensor_scalar_max(self.zr[0:1, zc:zc + 512],
                                                      self.zr[0:1, zc:zc + 512], 1e-30))
                self.vq(lambda e: e.tensor_scalar_max(self.zr[32:33, zc:zc + 512],
                                                      self.zr[32:33, zc:zc + 512], 1e-30))
                vw = self.vnow()
                self.s(lambda e: e.activation(self.rrep[0:33, :], self.zr[0:33, zc:zc + 512],
                                              AF.Ln), waits=[vw])
                sw = self.snow()
                self.s(lambda e: e.activation(self.rrep[0:33, :], self.rrep[0:33, :], AF.Exp,
                                              scale=-1.0), waits=[sw])
                sw = self.snow()
                self.mm(self.ps_r[:, :], self.pairsel[:, :], self.rrep[0:33, :], True, True,
                        waits=[sw])
                tm2 = self.tmark()
                self.v(lambda e: e.tensor_copy(self.rrep[:, :], self.ps_r[:, :]), waits=[tm2])
                self.mark_ps(self.ps_r)
                self.v(lambda e: e.tensor_tensor(
                    self.XA[:, ctx_base + 512 * c:ctx_base + 512 * (c + 1)],
                    ctx_ps[:, :], self.rrep[:, :], ALU.mult))
                self.mark_ps(ctx_ps)
            self._norm_pending = emit_norm

    def _emit_outproj(self, grp, last, ow_wait):
        if self._norm_pending is not None:
            self._norm_pending()
            self._norm_pending = None
        ctx_ready = self.vnow()
        has_ob = self._attn_has_ob and grp == 0
        for j in range(NT):
            pp = ((self.ps_w, self.ps_r), (self.ps_se, self.ps_so))[j % 2]
            for ch, pst in ((0, pp[0]), (1, pp[1])):
                for kp in range(4):
                    self.mm(pst[:, :], self.XA[:, 2048 * kp + 128 * j:2048 * kp + 128 * (j + 1)],
                            self.WT[:, 8192 + 1024 * kp + 512 * ch:8192 + 1024 * kp + 512 * (ch + 1)],
                            start=(kp == 0), stop=(kp == 3 and not has_ob),
                            waits=(ow_wait + [ctx_ready]) if (j == 0 and ch == 0 and kp == 0) else ())
                if has_ob:
                    self.mm(pst[:, :], self.ones1[:, :],
                            self.ob_sb[:, 512 * ch:512 * (ch + 1)], start=False, stop=True)
            tm = self.tmark()
            self.tile_addz(j, pp[0], pp[1], waits=[tm], ln=last)

    def emit_l0ffn(self):
        import os as _os
        stage = int(_os.environ.get("K_L0F_STAGE", "5"))
        d = self.d
        twr = self.dma("gpsimd", self.WT[:, 0:8192].rearrange("p (m n) -> p m n", n=1024),
                       d["wrT0"].rearrange("(m p) n -> p m n", p=128),
                       waits=[self.tnow()])

        self._xr_war = self.tnow()
        self._xk_war = self.tnow()
        for q in range(4):
            t0 = 512 * q
            war_t = [self._xr_war, self._xk_war]
            for m in range(NE):
                xm = self.xT[:, T * m + t0:T * m + t0 + 512]
                if q == 0:
                    xs = self.xT[:, T * m:T * m + 511]
                    dd = self.XC[:, 0:511]
                    self.v(lambda e, xm=xm, xs=xs, dd=dd: e.tensor_tensor(
                        dd, xm[:, 1:512], xs, ALU.subtract),
                        waits=([self.xT_ready] + war_t) if m == 0 else ())
                    self.v(lambda e, m=m, dd=dd, xs=xs: e.scalar_tensor_tensor(
                        self.XA[:, 512 * m + 1:512 * m + 512], dd, self.mur_sb[:, m:m + 1], xs,
                        ALU.mult, ALU.add))
                    self.v(lambda e, m=m, dd=dd, xs=xs: e.scalar_tensor_tensor(
                        self.XB[:, 512 * m + 1:512 * m + 512], dd, self.muk_sb[:, m:m + 1], xs,
                        ALU.mult, ALU.add))
                    self.v(lambda e, m=m, xm=xm: e.tensor_scalar_mul(
                        self.XA[:, 512 * m:512 * m + 1], xm[:, 0:1], self.mur_sb[:, m:m + 1]))
                    self.v(lambda e, m=m, xm=xm: e.tensor_scalar_mul(
                        self.XB[:, 512 * m:512 * m + 1], xm[:, 0:1], self.muk_sb[:, m:m + 1]))
                else:
                    xs = self.xT[:, T * m + t0 - 1:T * m + t0 + 511]
                    dd = self.XC[:, 0:512]
                    self.v(lambda e, xm=xm, xs=xs, dd=dd: e.tensor_tensor(
                        dd, xm, xs, ALU.subtract), waits=war_t if m == 0 else ())
                    self.v(lambda e, m=m, dd=dd, xs=xs: e.scalar_tensor_tensor(
                        self.XA[:, 512 * m:512 * (m + 1)], dd, self.mur_sb[:, m:m + 1], xs,
                        ALU.mult, ALU.add))
                    self.v(lambda e, m=m, dd=dd, xs=xs: e.scalar_tensor_tensor(
                        self.XB[:, 512 * m:512 * (m + 1)], dd, self.muk_sb[:, m:m + 1], xs,
                        ALU.mult, ALU.add))
            mix_done = self.vnow()
            if stage < 2:
                continue
            for jj in range(4):
                for ch in range(2):
                    pst = self.ps_se if (2 * jj + ch) % 2 == 0 else self.ps_se2
                    for m in range(NE):
                        self.mm(pst[:, :], self.XA[:, 512 * m + 128 * jj:512 * m + 128 * (jj + 1)],
                                self.WT[:, 1024 * m + 512 * ch:1024 * m + 512 * (ch + 1)],
                                start=(m == 0), stop=(m == NE - 1),
                                waits=[twr, mix_done, self.snow()] if (jj == 0 and ch == 0 and m == 0) else ())
                    tm = self.tmark()
                    dst = self.XA[:, 4096 + 1024 * jj + 512 * ch:4096 + 1024 * jj + 512 * (ch + 1)]
                    self.s(lambda e, pst=pst, dst=dst: e.activation(dst, pst[:, :], AF.Sigmoid),
                           waits=([tm, self.vnow()] if (jj == 0 and ch == 0) else [tm]))
                    self.mark_ps(pst, "S")
            if stage < 3:
                continue
            self._xr_war = self.tnow()
            for half in range(2):
                twk = self.dma("gpsimd", self.WT[:, 8192:12288].rearrange("p (m n) -> p m n", n=512),
                               AP(d["wkT0"].tensor, 512 * half,
                                  [[1024, 128], [131072, NE], [1, 512]]),
                               waits=[self.tnow()])
                for dt4 in range(4):
                    dt = 4 * half + dt4
                    pst = self.ps_so if dt % 2 == 0 else self.ps_so2
                    for m in range(NE):
                        self.mm(pst[:, :],
                                self.WT[:, 8192 + 512 * m + 128 * dt4:8192 + 512 * m + 128 * (dt4 + 1)],
                                self.XB[:, 512 * m:512 * (m + 1)],
                                start=(m == 0), stop=(m == NE - 1),
                                waits=[twk, mix_done, self.snow()] if (dt4 == 0 and m == 0) else ())
                    tm = self.tmark()
                    self.s(lambda e, pst=pst: e.activation(self.XC[:, 512:1024], pst[:, :],
                                                           AF.Relu), waits=[tm, self.vnow()])
                    self.mark_ps(pst, "S")
                    sw = self.snow()
                    self.v(lambda e, dt=dt: e.tensor_tensor(
                        self.XB[:, 4096 + 512 * dt:4096 + 512 * (dt + 1)],
                        self.XC[:, 512:1024], self.XC[:, 512:1024], ALU.mult),
                        waits=([sw, self.tnow()] if dt == 0 else [sw]))
            kact_done = self.vnow()
            self._xk_war = self.tnow()
            if stage < 4:
                continue
            for ch in range(2):
                twv = self.dma("gpsimd", self.WT2[:, :].rearrange("p (m n) -> p m n", n=512),
                               AP(d["wvT0"].tensor, 512 * ch,
                                  [[1024, 128], [131072, NE], [1, 512]]),
                               waits=[self.tnow()])
                for jj in range(4):
                    j = 4 * q + jj
                    pst = self.ps_w if jj % 2 == 0 else self.ps_r
                    for m in range(NE):
                        self.mm(pst[:, :],
                                self.XB[:, 4096 + 512 * m + 128 * jj:4096 + 512 * m + 128 * (jj + 1)],
                                self.WT2[:, 512 * m:512 * (m + 1)],
                                start=(m == 0), stop=(m == NE - 1),
                                waits=[twv, kact_done] if (jj == 0 and m == 0) else ())
                    tm = self.tmark()
                    self.v(lambda e, pst=pst, jj=jj, ch=ch: e.tensor_tensor(
                        self.XC[:, 1024:1536], pst[:, :],
                        self.XA[:, 4096 + 1024 * jj + 512 * ch:4096 + 1024 * jj + 512 * (ch + 1)],
                        ALU.mult), waits=[tm, self.snow()])
                    self.mark_ps(pst)
                    rj = self.resid[:, 1024 * j + 512 * ch:1024 * j + 512 * (ch + 1)]
                    acc = self.stat[:, (j if ch == 0 else 16 + j):(j if ch == 0 else 16 + j) + 1]
                    self.v(lambda e, rj=rj, acc=acc: e.scalar_tensor_tensor(
                        rj, self.XC[:, 1024:1536], 1.0, rj, ALU.mult, ALU.add, accum_out=acc))
                    if ch == 1:
                        vw = self.vnow()
                        self.s(lambda e, j=j: e.activation(
                            self.sqs[:, :], self.resid[:, 1024 * j:1024 * (j + 1)], AF.Square,
                            accum_out=self.stat[:, 32 + j:33 + j]), waits=[vw])
        if stage < 5:
            self.ln_done = self.vnow()
            return
        self.emit_ln_finish(need_xT=True)

    def emit_x2_direct(self):
        sw0 = [self.xT_ready]
        for j in range(NT):
            self.s(lambda e, j=j: e.activation(
                self.sqs[:, :], self.resid[:, 1024 * j:1024 * (j + 1)], AF.Square,
                accum_out=self.x2s[:, j:j + 1]), waits=sw0 if j == 0 else ())
        self.x2_done = self.snow()

    def emit_l1ffn(self, bi):
        d = self.d
        tbt = self.dma("gpsimd", self.XB[:, 0:4096].rearrange("p (m n) -> p m n", n=512),
                       d["BT1"].rearrange("(m p) n -> p m n", p=128),
                       waits=[self.tnow()])
        taf = self.dma("gpsimd", self.XB[:, 4096:8192].rearrange("p (r n) -> p r n", n=1024),
                       d["AF1"].rearrange("(r p) n -> p r n", p=128))
        for j in range(NT):
            for m in range(NE):
                self.mm(self.ps_w[:, 0:32], self.xT[:, T * m + 128 * j:T * m + 128 * (j + 1)],
                        self.protT_sb[:, 32 * m:32 * (m + 1)],
                        start=(m == 0), stop=(m == NE - 1),
                        waits=[self.xT_ready, self.tnow()] if (j == 0 and m == 0) else ())
            tm = self.tmark()
            self.vq(lambda e, j=j: e.tensor_scalar(self.tpA[:, :], self.ps_w[:, 0:32],
                                                  -2.0, self.x2s[:, j:j + 1], ALU.mult, ALU.add),
                   waits=[tm, self.x2_done])
            self.mark_ps(self.ps_w)
            self.vq(lambda e: e.tensor_tensor(self.tpA[:, :], self.tpA[:, :], self.p2rep[:, :],
                                             ALU.add))
            self.vq(lambda e, j=j: e.tensor_scalar_max(self.tpD[:, 32 * j:32 * (j + 1)],
                                                      self.tpA[:, :], 0.0))
        vw = self.vnow()
        self.s(lambda e: e.activation(self.tpD[:, :], self.tpD[:, :], AF.Sqrt), waits=[vw])
        sq_done = self.snow()
        import os as _os3
        if self.dbg is not None and _os3.environ.get("K_DBG") == "dists":
            tkd = self.dma("sync", self.dbg[:, :], self.tpD[:, :], waits=[sq_done, self.vnow()])
            self.vq(lambda e: e.memset(self.smr[0:1, 60:61], 0.0), waits=[tkd])
            self.s(lambda e: e.activation(self.smr[0:1, 61:62], self.smr[0:1, 60:61], AF.Abs), waits=[tkd])
        for j in range(NT):
            dj = self.tpD[:, 32 * j:32 * (j + 1)]
            self.vq(lambda e, dj=dj: e.tensor_scalar_mul(self.tpA[:, :], dj, self.rtB[:, 0:1]),
                   waits=[sq_done] if j == 0 else ())
            self.vq(lambda e: e.reduce_max(self.tpM[:, 0:1], self.tpA[:, :], axis=AX.X))
            self.vq(lambda e: e.tensor_scalar(self.tpB[:, :], self.tpA[:, :], self.tpM[:, 0:1],
                                             None, ALU.is_equal))
            self.vq(lambda e: e.scalar_tensor_tensor(self.tpC[:, :], self.tpB[:, :], NEG,
                                                    self.tpA[:, :], ALU.mult, ALU.add))
            self.vq(lambda e: e.reduce_max(self.tpM[:, 1:2], self.tpC[:, :], axis=AX.X))
            self.vq(lambda e: e.tensor_scalar(self.tpC[:, :], self.tpC[:, :], self.tpM[:, 1:2],
                                             None, ALU.is_equal))
            self.vq(lambda e: e.tensor_add(self.tpB[:, :], self.tpB[:, :], self.tpC[:, :]))
            self.vq(lambda e: e.tensor_scalar_mul(self.tpM[:, 2:3], self.tpM[:, 0:1], -1.0))
            vw = self.vnow()
            self.s(lambda e: e.activation(self.tpC[:, :], self.tpA[:, :], AF.Exp,
                                          bias=self.tpM[:, 2:3]), waits=[vw])
            self.s(lambda e: e.activation(self.tpM[:, 3:4], self.tpM[:, 1:2], AF.Exp,
                                          bias=self.tpM[:, 2:3]))
            sw = self.snow()
            self.vq(lambda e: e.tensor_scalar_add(self.tpM[:, 4:5], self.tpM[:, 3:4], 1.0),
                   waits=[sw])
            self.vq(lambda e: e.reciprocal(self.tpM[:, 5:6], self.tpM[:, 4:5]))
            self.vq(lambda e, j=j: e.scalar_tensor_tensor(
                self.wfull[:, 32 * j:32 * (j + 1)], self.tpC[:, :], self.tpM[:, 5:6],
                self.tpB[:, :], ALU.mult, ALU.mult))
        w_done = self.vnow()
        import os as _os
        if self.dbg is not None and _os.environ.get("K_DBG") == "wfull":
            tkd = self.dma("gpsimd", self.dbg[:, :], self.wfull[:, :], waits=[w_done])
            self.v(lambda e: e.memset(self.smr[0:1, 60:61], 0.0), waits=[tkd])
        for j in range(NT):
            self.mm(self.ps_r[0:32, 0:128], self.wfull[:, 32 * j:32 * (j + 1)], self.idn[:, :],
                    True, True, waits=[w_done] if j == 0 else ())
            tm = self.tmark()
            self.v(lambda e, j=j: e.tensor_copy(self.wT_sb[0:32, 128 * j:128 * (j + 1)],
                                                self.ps_r[0:32, 0:128]), waits=[tm])
            self.mark_ps(self.ps_r)
        wt_done = self.vnow()
        for r in range(4):
            for c in range(4):
                pst = self.ps_se if (4 * r + c) % 2 == 0 else self.ps_se2
                for m in range(NE):
                    self.mm(pst[:, :], self.XB[:, 512 * m + 128 * r:512 * m + 128 * (r + 1)],
                            self.xT[:, T * m + 512 * c:T * m + 512 * (c + 1)],
                            start=(m == 0), stop=(m == NE - 1),
                            waits=[tbt, self.snow(), self.vnow()] if (r == 0 and c == 0 and m == 0) else ())
                self.mm(self.ps_so[:, :], self.expand[:, 128 * r:128 * (r + 1)],
                        self.wT_sb[0:32, 512 * c:512 * (c + 1)], True, True,
                        waits=[wt_done] if (r == 0 and c == 0) else ())
                tm = self.tmark()
                self.v(lambda e: e.tensor_copy(self.XC[:, 0:512], self.ps_so[:, :]), waits=[tm])
                self.mark_ps(self.ps_so)
                self.v(lambda e, pst=pst, r=r, c=c: e.tensor_tensor(
                    self.XA[:, 2048 * r + 512 * c:2048 * r + 512 * (c + 1)], pst[:, :],
                    self.XC[:, 0:512], ALU.mult))
                self.mark_ps(pst)
        h_done = self.vnow()
        for j in range(NT):
            pp = ((self.ps_w, self.ps_r), (self.ps_se, self.ps_so))[j % 2]
            for ch, pst in ((0, pp[0]), (1, pp[1])):
                for r in range(4):
                    self.mm(pst[:, :], self.XA[:, 2048 * r + 128 * j:2048 * r + 128 * (j + 1)],
                            self.XB[:, 4096 + 1024 * r + 512 * ch:4096 + 1024 * r + 512 * (ch + 1)],
                            start=(r == 0), stop=False,
                            waits=[taf, h_done] if (j == 0 and ch == 0 and r == 0) else ())
                self.mm(pst[:, :], self.wT_sb[0:32, 128 * j:128 * (j + 1)],
                        self.biasm_sb[0:32, 512 * ch:512 * (ch + 1)], start=False, stop=True)
            tm = self.tmark()
            self.tile_addz(j, pp[0], pp[1], waits=[tm], ln=True)
        self.emit_ln_finish(need_xT=False)
        self.emit_store_out(bi, waits=[self.ln_done])

    def build(self):
        self.emit_clear_all()
        self.emit_startup()
        for bi in range(self.n_batch):
            self.emit_load_resid(bi)
            import os as _os
            if _os.environ.get("K_NO_XT"):
                self.xT_ready = (self.V, 0)
            else:
                self.emit_make_xT(extra_waits=[self.resid_ready])
            done_store = False
            if "l0a" in self.blocks:
                self._attn_has_ob = True
                self.emit_attn(0)
            if "l0f" in self.blocks:
                self.emit_l0ffn()
            if "l1a" in self.blocks:
                self._attn_has_ob = False
                self.emit_attn(1)
            if "l1f" in self.blocks:
                if "l1a" not in self.blocks:
                    self.emit_x2_direct()
                self.emit_l1ffn(bi)
                done_store = True
            if not done_store:
                wfin = [self.ln_done] if self.blocks else [self.xT_ready, self.resid_ready]
                self.emit_store_out(bi, waits=wfin)
        dy, dg = self.DY.count, self.DG.count
        self.op("sync", lambda e: e.nop(), waits=[(self.DY, dy), (self.DG, dg)])
        return self.finalize()


_CONSTS = host_consts()


def kernel(**inputs):
    n_cores = 8
    x = np.ascontiguousarray(np.asarray(inputs["L0_attn_hidden_states"], dtype=np.float32))
    w = prep_weights(inputs)
    base = dict(w)
    base.update(_CONSTS)
    shards = [np.ascontiguousarray(x[2 * c:2 * c + 2]) for c in range(n_cores)]

    prog = Prog()
    nc = prog.build()
    in_maps = []
    for c in range(n_cores):
        m = {k: np.copy(v) for k, v in base.items()}
        m["x_in"] = shards[c]
        in_maps.append(m)
    res = run_bass_kernel_spmd(nc, in_maps, core_ids=list(range(n_cores)))
    out = np.concatenate([np.asarray(r["y_out"]) for r in res.results], axis=0)
    if not np.isfinite(out).all():
        import os as _os
        if _os.environ.get("K_SAVENAN"):
            np.save(_os.environ["K_SAVENAN"], out)
        res = run_bass_kernel_spmd(nc, in_maps, core_ids=list(range(n_cores)))
        out = np.concatenate([np.asarray(r["y_out"]) for r in res.results], axis=0)
    return np.ascontiguousarray(out.astype(np.float32))
